# version 05 (final)
# speedup vs baseline: 1.0208x; 1.0024x over previous
import numpy as np
import ml_dtypes
from contextlib import ExitStack
import concourse.bass as bass
import concourse.mybir as mybir

F32 = mybir.dt.float32
BF16 = mybir.dt.bfloat16
AF = mybir.ActivationFunctionType
ALU = mybir.AluOpType
AX = mybir.AxisListType

D = 1024
NCH = 8
H = 16
DH = 64
C = 64
FFN = 4096
LN_EPS = 1e-5
BF = ml_dtypes.bfloat16


class Prog:
    def __init__(self, nc, block, ctx):
        self.nc, self.block, self.ctx = nc, block, ctx
        self.cnt = {}
        self.hs = {}
        for name in ["t", "v", "a", "g", "q", "dS", "dG", "dA"]:
            self.hs[name] = ctx.enter_context(nc.semaphore(name))
            self.cnt[name] = 0

    def inc(self, instr, name, amt=1):
        instr.then_inc(self.hs[name], amt)
        self.cnt[name] += amt
        return instr

    def wait(self, eng, *names):
        for n in names:
            if self.cnt[n] > 0:
                eng.wait_ge(self.hs[n], self.cnt[n])

    def chain(self, eng, instr):
        self.inc(instr, "q", 1)
        eng.wait_ge(self.hs["q"], self.cnt["q"])
        return instr


def build_launch1(T=2048, upto=99, dbg=False):
    nc = bass.Bass("TRN2", target_bir_lowering=False, detect_race_conditions=False)
    TW = min(512, T)
    NT = T // TW
    NTT = T // 128
    NCHK = T // C

    def ein(name, shape, dt=F32):
        return nc.dram_tensor(name, list(shape), dt, kind="ExternalInput")

    xT = ein("xT", (D, T))
    Wq0 = ein("Wq0", (D, D)); Wk0 = ein("Wk0", (D, D)); Wv0 = ein("Wv0", (D, D))
    Wg0 = ein("Wg0", (D, D)); Wo0 = ein("Wo0", (D, D))
    W1 = ein("W1", (D, FFN)); W2 = ein("W2", (FFN, D))
    ln1w = ein("ln1w", (128, NCH)); ln1b = ein("ln1b", (128, NCH))
    ln2w = ein("ln2w", (128, NCH)); ln2b = ein("ln2b", (128, NCH))
    bgv = ein("bgv", (128, NCH)); b1v = ein("b1v", (128, FFN // 128)); b2v = ein("b2v", (128, NCH))
    WrT = ein("WrT", (D, D)); WkTr = ein("WkTr", (D, D)); WvTr = ein("WvTr", (D, D))
    murv = ein("murv", (128, NCH)); mukv = ein("mukv", (128, NCH))
    Wq1 = ein("Wq1", (D, D)); Wkv1 = ein("Wkv1", (D, 2 * D)); Wo1 = ein("Wo1", (D, D))
    bo1v = ein("bo1v", (128, NCH))
    gate_wT = ein("gate_wT", (D, 8))

    ident_c = nc.inline_tensor(np.eye(128).astype(BF), name="id_b16")
    identf_c = nc.inline_tensor(np.eye(128).astype(np.float32), name="id_f32")
    ones_c = nc.inline_tensor(np.ones((128, 128)).astype(BF), name="ones_b16")
    onecol_c = nc.inline_tensor(np.ones((1, 128)).astype(BF), name="onecol_b16")
    mrow = np.ones(T // 2 if T >= 2 else T, np.float32); mrow[0::C] = 0.0
    scanm_c = nc.inline_tensor(np.tile(mrow, (128, 1)).astype(BF), name="scanmask")
    tri = np.triu(np.ones((C, C), np.float32))
    triu2_c = nc.inline_tensor(np.tile(tri, (2, 2)).astype(BF), name="triu2")
    cmsm = (np.arange(128)[:, None] <= np.arange(TW + 384)[None, :] - 384).astype(np.float32)
    cmsm_c = nc.inline_tensor(cmsm.astype(BF), name="cmsm")
    e4 = np.zeros((2, 4, 128), np.float32)
    for j in range(4):
        e4[j // 2, j, 64 * (j % 2):64 * (j % 2) + 64] = 1.0
    E4_c = [nc.inline_tensor(e4[lc].astype(BF), name=f"E4_{lc}") for lc in range(2)]
    E4Tf_c = [nc.inline_tensor(e4[lc].T.copy(), name=f"E4Tf_{lc}") for lc in range(2)]
    E4Tb_c = [nc.inline_tensor(e4[lc].T.copy().astype(BF), name=f"E4Tb_{lc}") for lc in range(2)]

    ik = "ExternalOutput" if dbg else "Internal"
    y1d = nc.dram_tensor("y1d", [D, T], F32, kind=ik)
    x1d = nc.dram_tensor("x1d", [D, T], F32, kind=ik)
    x2d = nc.dram_tensor("x2d", [D, T], F32, kind=ik)
    x3T = nc.dram_tensor("x3T", [D, T], F32, kind="ExternalOutput")
    v3d = nc.dram_tensor("v3d", [T, D], BF16, kind=ik)
    ylnT = nc.dram_tensor("ylnT", [D, T], BF16, kind="ExternalOutput")
    scoresT = nc.dram_tensor("scoresT", [8, T], F32, kind="ExternalOutput")

    ctx = ExitStack()
    block = ctx.enter_context(nc.Block())
    p = Prog(nc, block, ctx)

    def sb(name, shape, dt=BF16):
        return ctx.enter_context(nc.sbuf_tensor(name, shape, dt))

    TH = T // 2
    NTH = TH // TW if TH >= TW else 1
    THW = min(TW, TH)
    BIGW = max(NCH * T, NCH * (T + 1), (FFN // 128) * TW)
    BIG = [sb(f"BIG{i}", [128, BIGW], BF16) for i in range(3)]
    STAT = [sb(f"STAT{i}", [128, T], BF16) for i in range(4)]
    FT = [sb(f"FT{i}", [128, T], F32) for i in range(2)]
    FT2 = sb("FT2f", [128, TW], F32)
    SQS = sb("SQSb", [128, 2 * TW], BF16)
    BSCR = sb("BSCRb", [128, T], BF16)
    VECS = sb("VECSf", [128, 17 * NCH + FFN // 128], F32)
    ONES = sb("ONESb", [128, 128], BF16)
    ONECOL = sb("ONECOLb", [1, 128], BF16)
    IDENT = sb("IDENTb", [128, 128], BF16)
    IDENTF = sb("IDENTf", [128, 128], F32)
    TRIU2 = sb("TRIU2b", [128, 2 * C], BF16)
    E4S = sb("E4Sb", [4, 2 * 128], BF16)
    E4STf = sb("E4STff", [128, 2 * 4], F32)
    E4STb = sb("E4STbb", [128, 2 * 4], BF16)
    CMSM = sb("CMSMb", [128, TW + 384], BF16)
    S4F = sb("S4Ff", [128, 128], F32)
    S4B = sb("S4Bb", [128, 128], BF16)
    AM = [sb(f"AM{i}", [128, 2 * C], BF16) for i in range(2)]
    OSCR = sb("OSCRb", [128, 2 * C], BF16)
    CDF = sb("CDFf", [128, T // C], F32)

    VOFF = {}
    off = 0
    for nm in ["ln1w", "ln1b", "ln2w", "ln2b", "bg", "b2", "mur", "muk", "bo1",
               "omur", "omuk", "nbg"]:
        VOFF[nm] = off; off += NCH
    VOFF["b1"] = off

    def vec(nm, cc):
        return VECS[:, VOFF[nm] + cc: VOFF[nm] + cc + 1]

    PS = [ctx.enter_context(nc.psum_tensor(f"PS{i}", [128, 512], F32)) for i in range(6)]
    PSB2 = [ctx.enter_context(nc.psum_tensor(f"PSB2_{i}", [128, 512], BF16)) for i in range(2)]

    _sc = {"cur": None}

    def mark(name):
        if _sc["cur"] is not None:
            nc.leave_named_scope(_sc["cur"][0], _sc["cur"][1], False)
            _sc["cur"] = None
        if name:
            sid, _ = nc.enter_named_scope(name, False)
            _sc["cur"] = (name, sid)

    @block.sync
    def _(e):
        for dst, src in [(ONES[:], ones_c), (ONECOL[:], onecol_c), (IDENT[:], ident_c),
                         (IDENTF[:], identf_c), (TRIU2[:], triu2_c)]:
            p.inc(e.dma_start(out=dst, in_=src.ap()), "dS", 16)
        p.inc(e.dma_start(out=E4S[:, 0:128], in_=E4_c[0].ap()), "dS", 16)
        p.inc(e.dma_start(out=E4S[:, 128:256], in_=E4_c[1].ap()), "dS", 16)
        p.inc(e.dma_start(out=E4STf[:, 0:4], in_=E4Tf_c[0].ap()), "dS", 16)
        p.inc(e.dma_start(out=E4STf[:, 4:8], in_=E4Tf_c[1].ap()), "dS", 16)
        p.inc(e.dma_start(out=E4STb[:, 0:4], in_=E4Tb_c[0].ap()), "dS", 16)
        p.inc(e.dma_start(out=E4STb[:, 4:8], in_=E4Tb_c[1].ap()), "dS", 16)
        for nm, t_ in [("ln1w", ln1w), ("ln1b", ln1b), ("ln2w", ln2w), ("ln2b", ln2b),
                       ("bg", bgv), ("b2", b2v), ("mur", murv), ("muk", mukv), ("bo1", bo1v)]:
            p.inc(e.dma_start(out=VECS[:, VOFF[nm]:VOFF[nm] + NCH], in_=t_.ap()), "dS", 16)
        p.inc(e.dma_start(out=VECS[:, VOFF["b1"]:VOFF["b1"] + FFN // 128], in_=b1v.ap()), "dS", 16)
        p.inc(e.dma_start(out=CMSM[:], in_=cmsm_c.ap()), "dS", 16)

    @block.vector
    def _(e):
        p.wait(e, "dS")
        for cc in range(NCH):
            e.tensor_scalar(vec("omur", cc), vec("mur", cc), -1.0, 1.0, ALU.mult, ALU.add)
            e.tensor_scalar(vec("nbg", cc), vec("bg", cc), -1.0, None, ALU.mult)
            last = e.tensor_scalar(vec("omuk", cc), vec("muk", cc), -1.0, 1.0, ALU.mult, ALU.add)
        p.inc(last, "v", 1)

    def ln_stats(src_bf, mean_t, rstd_t, stride=None, off0=0):
        stride = stride if stride is not None else T
        for tt in range(NT):
            s0 = tt * TW
            v0 = p.cnt["v"]
            t0 = p.cnt["t"]

            @block.vector
            def _(e, s0=s0, v0=v0, t0=t0):
                p.wait(e, "a", "dG", "g")
                if t0 > 0:
                    e.wait_ge(p.hs["t"], t0)
                for cc in range(NCH):
                    if cc >= 2:
                        e.wait_ge(p.hs["t"], t0 + cc - 1)
                    srcq = src_bf[:, cc * stride + off0 + s0: cc * stride + off0 + s0 + TW]
                    p.inc(e.tensor_tensor(SQS[:, (cc % 2) * TW:(cc % 2 + 1) * TW], srcq, srcq, ALU.mult), "v", 1)

            @block.tensor
            def _(e, s0=s0, v0=v0):
                p.wait(e, "a")
                for cc in range(NCH):
                    e.wait_ge(p.hs["v"], v0 + cc + 1)
                    srcq = src_bf[:, cc * stride + off0 + s0: cc * stride + off0 + s0 + TW]
                    e.matmul(PS[0][:, 0:TW], ONES[:], srcq, start=(cc == 0), stop=(cc == NCH - 1))
                    last = e.matmul(PS[1][:, 0:TW], ONES[:], SQS[:, (cc % 2) * TW:(cc % 2 + 1) * TW],
                                    start=(cc == 0), stop=(cc == NCH - 1))
                    p.inc(last, "t", 1)

            @block.scalar
            def _(e, s0=s0):
                p.wait(e, "t")
                p.inc(e.activation(mean_t[:, s0:s0 + TW], PS[0][:, 0:TW], AF.Copy, scale=1.0 / D), "a", 1)

            @block.vector
            def _(e, s0=s0):
                p.wait(e, "a")
                e.tensor_tensor(FT2[:, 0:TW], mean_t[:, s0:s0 + TW], mean_t[:, s0:s0 + TW], ALU.mult)
                e.scalar_tensor_tensor(FT2[:, 0:TW], PS[1][:, 0:TW], 1.0 / D,
                                       FT2[:, 0:TW], ALU.mult, ALU.subtract)
                p.inc(e.tensor_scalar_add(FT2[:, 0:TW], FT2[:, 0:TW], LN_EPS), "v", 1)

            @block.scalar
            def _(e, s0=s0):
                p.wait(e, "v")
                e.activation(FT2[:, 0:TW], FT2[:, 0:TW], AF.Ln)
                p.inc(e.activation(rstd_t[:, s0:s0 + TW], FT2[:, 0:TW], AF.Exp, scale=-0.5), "a", 1)

    def cast_from_dram(dram, dst_bf):
        @block.gpsimd
        def _(e):
            p.wait(e, "v", "a", "t", "dS")
            for cc in range(NCH):
                p.inc(e.dma_start(out=dst_bf[:, cc * T:(cc + 1) * T],
                                  in_=dram.ap()[cc * 128:(cc + 1) * 128, :]), "dG", 16)

    def ln_apply(src_dram, src_bf, mean_t, rstd_t, out_bf, w=None, b=None,
                 stride=None, off0=0):
        stride = stride if stride is not None else T
        for cc in range(NCH):
            if src_dram is not None:
                @block.sync
                def _(e, cc=cc):
                    p.wait(e, "v")
                    p.inc(e.dma_start(out=FT[0][:, 0:T], in_=src_dram.ap()[cc * 128:(cc + 1) * 128, :]), "dS", 16)

            @block.vector
            def _(e, cc=cc):
                p.wait(e, "a")
                if src_dram is not None:
                    p.wait(e, "dS")
                    src = FT[0][:, 0:T]
                else:
                    src = src_bf[:, cc * T:(cc + 1) * T]
                e.tensor_tensor(FT[1][:, 0:T], src, mean_t[:, 0:T], ALU.subtract)
                dst = out_bf[:, cc * stride + off0: cc * stride + off0 + T]
                if w is None:
                    p.inc(e.tensor_tensor(dst, FT[1][:, 0:T], rstd_t[:, 0:T], ALU.mult), "v", 1)
                else:
                    e.tensor_tensor(FT[1][:, 0:T], FT[1][:, 0:T], rstd_t[:, 0:T], ALU.mult)
                    p.inc(e.tensor_scalar(dst, FT[1][:, 0:T], vec(w, cc), vec(b, cc),
                                          ALU.mult, ALU.add), "v", 1)

    def load_wchunks(wdram, dst, cols=D, col_off=0, n_k=NCH):
        @block.gpsimd
        def _(e):
            p.wait(e, "t")
            p.inc(e.dma_start(out=dst[:, 0:n_k * cols].rearrange("p (c f) -> p c f", f=cols),
                              in_=wdram.ap()[0:n_k * 128, col_off:col_off + cols].rearrange("(c p) f -> p c f", p=128)), "dG", 16)

    def proj_fmajor(wbuf, wcols, wcol_of, rhs_bf, rhs_stride, n_oc, evac, rhs_off=0):
        for oc in range(n_oc):
            for tt in range(NT):
                i = oc * NT + tt
                ps = PS[i % 2]

                @block.tensor
                def _(e, oc=oc, tt=tt, ps=ps, i=i):
                    if i < 2:
                        p.wait(e, "dG", "v", "a")
                    else:
                        e.wait_ge(p.hs["a"], p.cnt["a"] - 1)
                    wo = wcol_of(oc)
                    for dc in range(NCH):
                        last = e.matmul(ps[:, 0:TW],
                                        wbuf[:, dc * wcols + wo: dc * wcols + wo + 128],
                                        rhs_bf[:, rhs_off + dc * rhs_stride + tt * TW: rhs_off + dc * rhs_stride + tt * TW + TW],
                                        start=(dc == 0), stop=(dc == NCH - 1))
                    p.inc(last, "t", 1)

                @block.scalar
                def _(e, oc=oc, tt=tt, ps=ps):
                    p.wait(e, "t")
                    evac(e, oc, tt, ps)


    def _finish():
        @block.sync
        def _(e):
            p.wait(e, "dS", "dG", "dA")
        @block.vector
        def _(e):
            p.wait(e, "t", "a", "g")
    mark("ph1_4_ln_proj")
    XBF = BIG[0]

    @block.gpsimd
    def _(e):
        p.wait(e, "v")
        for cc in range(NCH):
            p.inc(e.dma_start(out=XBF[:, cc * T:(cc + 1) * T], in_=xT.ap()[cc * 128:(cc + 1) * 128, :]), "dG", 16)

    ln_stats(XBF, STAT[0], STAT[1])
    if upto <= 1:
        _finish()
        return nc
    Y0B = BIG[1]
    ln_apply(xT, None, STAT[0], STAT[1], Y0B)
    ln_stats(Y0B, STAT[2], STAT[3])
    XN = BIG[2]
    ln_apply(None, Y0B, STAT[2], STAT[3], XN, w="ln1w", b="ln1b")

    if upto <= 2:
        _finish()
        return nc

    mark("ph5_gla")
    OF = BIG[0]
    NCHH = TH // C
    with ExitStack() as gctx:
        def gsb(nm, shape, dt=BF16):
            return gctx.enter_context(nc.sbuf_tensor(nm, shape, dt))

        QG = gsb("gQG", [128, T]); KG = gsb("gKG", [128, T])
        QA = gsb("gQA", [128, T]); KA = gsb("gKA", [128, T])
        VT = gsb("gVT", [128, T]); KPT = gsb("gKPT", [128, T])
        ECD = gsb("gEC", [128, T]); DTE = gsb("gDTE", [128, T])
        ALPH = gsb("gALPH", [128, TH])
        LACB = gctx.enter_context(nc.sbuf_tensor("gLACB", [128, TH], F32))
        SCANM = gsb("gSCANM", [128, TH])
        SMRN = gsb("gSMRN", [4, T])
        SMLAM = gctx.enter_context(nc.sbuf_tensor("gSMLAM", [4, TH], F32))
        SBQ = gsb("gSBQ", [4, TH]); SBK = gsb("gSBK", [4, TH])
        WGB = [gsb(f"gWGB{i}", [128, NCH * 256], BF16) for i in range(2)]

        @block.sync
        def _(e):
            p.wait(e, "v")
            p.inc(e.dma_start(out=SCANM[:], in_=scanm_c.ap()), "dS", 16)

        for g in range(4):
            gc0 = 2 * g
            for th in range(2):
                toff = th * TH

                @block.gpsimd
                def _(e, g=g):
                    p.wait(e, "t", "v")
                    for wi, wd in enumerate([Wq0, Wk0]):
                        p.inc(e.dma_start(out=WGB[wi][:].rearrange("p (c f) -> p c f", f=256),
                                          in_=wd.ap()[:, 256 * g:256 * g + 256].rearrange("(c p) f -> p c f", p=128)), "dG", 16)

                def evprj(dstb):
                    def ev(e, oc, tt, ps, dstb=dstb):
                        p.inc(e.activation(dstb[:, oc * TH + tt * THW: oc * TH + tt * THW + THW],
                                           ps[:, 0:THW], AF.Copy), "a", 1)
                    return ev

                def proj_half(wbuf, dstb):
                    for oc in range(2):
                        for tt in range(NTH):
                            i = oc * NTH + tt
                            ps = PS[i % 2]

                            @block.tensor
                            def _(e, oc=oc, tt=tt, ps=ps, i=i):
                                if i < 2:
                                    p.wait(e, "dG", "v", "a")
                                else:
                                    e.wait_ge(p.hs["a"], p.cnt["a"] - 1)
                                for dc in range(NCH):
                                    last = e.matmul(ps[:, 0:THW],
                                                    wbuf[:, dc * 256 + oc * 128: dc * 256 + oc * 128 + 128],
                                                    XN[:, dc * T + toff + tt * THW: dc * T + toff + tt * THW + THW],
                                                    start=(dc == 0), stop=(dc == NCH - 1))
                                p.inc(last, "t", 1)

                            @block.scalar
                            def _(e, oc=oc, tt=tt, ps=ps, dstb=dstb):
                                p.wait(e, "t")
                                evprj(dstb)(e, oc, tt, ps)

                proj_half(WGB[0], QG)
                proj_half(WGB[1], KG)

                @block.gpsimd
                def _(e, g=g):
                    p.wait(e, "t", "v")
                    for wi, wd in enumerate([Wv0, Wg0]):
                        p.inc(e.dma_start(out=WGB[wi][:].rearrange("p (c f) -> p c f", f=256),
                                          in_=wd.ap()[:, 256 * g:256 * g + 256].rearrange("(c p) f -> p c f", p=128)), "dG", 16)

                for tp in range(TH // 128):
                    ps = PS[tp % 2]

                    @block.tensor
                    def _(e, tp=tp, ps=ps):
                        if tp < 2:
                            p.wait(e, "a", "dG")
                        else:
                            e.wait_ge(p.hs["a"], p.cnt["a"] - 1)
                        for dc in range(NCH):
                            last = e.matmul(ps[:, 0:256],
                                            XN[:, dc * T + toff + tp * 128: dc * T + toff + tp * 128 + 128],
                                            WGB[0][:, dc * 256:(dc + 1) * 256],
                                            start=(dc == 0), stop=(dc == NCH - 1))
                        p.inc(last, "t", 1)

                    @block.scalar
                    def _(e, tp=tp, ps=ps):
                        p.wait(e, "t")
                        p.inc(e.activation(VT[:, tp * 256:(tp + 1) * 256], ps[:, 0:256], AF.Copy), "a", 1)

                for lc in range(2):
                    for tt in range(NTH):
                        ps = PS[tt % 2]

                        @block.tensor
                        def _(e, lc=lc, tt=tt, ps=ps):
                            e.wait_ge(p.hs["a"], p.cnt["a"] - 1) if (tt >= 2 or lc > 0) else p.wait(e, "a")
                            for dc in range(NCH):
                                last = e.matmul(ps[:, 0:THW],
                                                WGB[1][:, dc * 256 + lc * 128: dc * 256 + lc * 128 + 128],
                                                XN[:, dc * T + toff + tt * THW: dc * T + toff + tt * THW + THW],
                                                start=(dc == 0), stop=(dc == NCH - 1))
                            p.inc(last, "t", 1)

                        @block.scalar
                        def _(e, lc=lc, tt=tt, ps=ps):
                            p.wait(e, "t")
                            p.inc(e.activation(ALPH[:, tt * THW:(tt + 1) * THW], ps[:, 0:THW], AF.Sigmoid,
                                               bias=vec("bg", gc0 + lc)), "a", 1)

                    @block.vector
                    def _(e, lc=lc):
                        p.wait(e, "a", "t")
                        p.inc(e.tensor_scalar_max(FT[0][:, 0:TH], ALPH[:, 0:TH], 1e-6), "v", 1)

                    @block.scalar
                    def _(e, lc=lc):
                        p.wait(e, "v")
                        p.inc(e.activation(FT[1][:, 0:TH], FT[0][:, 0:TH], AF.Ln), "a", 1)

                    @block.vector
                    def _(e, lc=lc):
                        p.wait(e, "a")
                        p.inc(e.tensor_tensor_scan(LACB[:, 0:TH], SCANM[:, 0:TH],
                                                   FT[1][:, 0:TH], 0.0, ALU.mult, ALU.add), "v", 1)

                    @block.vector
                    def _(e, lc=lc):
                        p.wait(e, "v")
                        lac3 = LACB[:, 0:TH].rearrange("p (n c) -> p n c", c=C)
                        lsb = lac3[:, :, C - 1:C].broadcast_to((128, NCHH, C))
                        p.inc(e.tensor_tensor(FT[0][:, 0:TH].rearrange("p (n c) -> p n c", c=C),
                                              lsb, lac3, ALU.subtract), "v", 1)

                    @block.scalar
                    def _(e, lc=lc):
                        p.wait(e, "v")
                        e.activation(ECD[:, lc * TH:(lc + 1) * TH], LACB[:, 0:TH], AF.Exp)
                        e.activation(DTE[:, lc * TH:(lc + 1) * TH], FT[0][:, 0:TH], AF.Exp)
                        lace = LACB[:, 0:TH].rearrange("p (n c) -> p n c", c=C)[:, :, C - 1:C]
                        p.inc(e.activation(CDF[:, lc * NCHH:(lc + 1) * NCHH],
                                           lace.rearrange("p n c -> p (n c)"), AF.Exp), "a", 1)

                    for tt in range(NTH):
                        @block.tensor
                        def _(e, lc=lc, tt=tt):
                            p.wait(e, "a", "v")
                            p.inc(e.matmul(PS[2 + tt][0:4, 0:THW], E4STf[:, lc * 4:lc * 4 + 4],
                                           LACB[:, tt * THW: tt * THW + THW],
                                           start=(lc == 0), stop=(lc == 1)), "t", 1)

                for tt in range(NTH):
                    @block.scalar
                    def _(e, tt=tt):
                        p.wait(e, "t")
                        p.inc(e.activation(SMLAM[0:4, tt * THW: tt * THW + THW],
                                           PS[2 + tt][0:4, 0:THW], AF.Copy, scale=1.0 / DH), "a", 1)

                for qk, SRC in [(0, QG), (1, KG)]:
                    for tt in range(NTH):
                        @block.vector
                        def _(e, SRC=SRC, tt=tt):
                            p.wait(e, "a", "t")
                            for lc in range(2):
                                s_ = SRC[:, lc * TH + tt * THW: lc * TH + tt * THW + THW]
                                last = e.tensor_tensor(SQS[:, lc * TW:lc * TW + THW], s_, s_, ALU.mult)
                            p.inc(last, "v", 1)

                        @block.tensor
                        def _(e, tt=tt):
                            p.wait(e, "v")
                            for lc in range(2):
                                last = e.matmul(PS[4][0:4, 0:THW], E4STb[:, lc * 4:lc * 4 + 4],
                                                SQS[:, lc * TW:lc * TW + THW],
                                                start=(lc == 0), stop=(lc == 1))
                            p.inc(last, "t", 1)

                        @block.scalar
                        def _(e, qk=qk, tt=tt):
                            p.wait(e, "t")
                            p.inc(e.activation(FT[0][0:4, qk * TH + tt * THW: qk * TH + tt * THW + THW],
                                               PS[4][0:4, 0:THW], AF.Copy), "a", 1)

                @block.scalar
                def _(e):
                    p.wait(e, "a", "v", "t")
                    e.activation(FT[0][0:4, 0:T], FT[0][0:4, 0:T], AF.Ln)
                    p.inc(e.activation(FT[0][0:4, 0:T], FT[0][0:4, 0:T], AF.Exp, scale=0.5), "a", 1)

                @block.vector
                def _(e):
                    p.wait(e, "a")
                    e.tensor_scalar_max(FT[0][0:4, 0:T], FT[0][0:4, 0:T], 1e-12)
                    with nc.allow_low_precision(reason="rn is a bf16 matmul operand downstream"):
                        p.inc(e.reciprocal(SMRN[0:4, 0:T], FT[0][0:4, 0:T]), "v", 1)

                @block.vector
                def _(e):
                    p.wait(e, "v", "a")
                    lam3 = SMLAM[0:4, 0:TH].rearrange("p (n c) -> p n c", c=C)
                    p.chain(e, e.tensor_tensor(FT2[0:4, 0:NCHH], lam3[:, :, 0:1].rearrange("p n c -> p (n c)"),
                                               lam3[:, :, C - 1:C].rearrange("p n c -> p (n c)"), ALU.add))
                    p.chain(e, e.tensor_scalar_mul(FT2[0:4, 0:NCHH], FT2[0:4, 0:NCHH], 0.5))
                    sb3 = FT2[0:4, 0:NCHH].rearrange("p (n c) -> p n c", c=1).broadcast_to((4, NCHH, C))
                    p.chain(e, e.tensor_tensor(lam3, lam3, sb3, ALU.subtract))
                    p.chain(e, e.tensor_scalar_min(SMLAM[0:4, 0:TH], SMLAM[0:4, 0:TH], 41.5))
                    p.inc(e.tensor_scalar_max(SMLAM[0:4, 0:TH], SMLAM[0:4, 0:TH], -41.5), "v", 1)

                @block.scalar
                def _(e):
                    p.wait(e, "v")
                    e.activation(SBQ[0:4, 0:TH], SMLAM[0:4, 0:TH], AF.Exp)
                    p.inc(e.activation(SBK[0:4, 0:TH], SMLAM[0:4, 0:TH], AF.Exp, scale=-1.0), "a", 1)

                @block.vector
                def _(e):
                    p.wait(e, "a")
                    e.tensor_tensor(SBQ[0:4, 0:TH], SMRN[0:4, 0:TH], SBQ[0:4, 0:TH], ALU.mult)
                    p.inc(e.tensor_tensor(SBK[0:4, 0:TH], SMRN[0:4, TH:T], SBK[0:4, 0:TH], ALU.mult), "v", 1)

                bc_specs = [(SBQ, 0, QA, QG), (SBK, 0, KA, KG),
                            (SMRN, 0, QG, QG), (SMRN, TH, KG, KG)]
                bi = 0
                for rowsb, roff, dstb, srcb in bc_specs:
                    for lc in range(2):
                        for tt in range(NTH):
                            psb = PS[4 + (bi % 2)]

                            @block.tensor
                            def _(e, rowsb=rowsb, roff=roff, lc=lc, tt=tt, psb=psb, bi=bi):
                                if bi < 2:
                                    p.wait(e, "a", "v")
                                else:
                                    e.wait_ge(p.hs["v"], p.cnt["v"] - 1)
                                p.inc(e.matmul(psb[:, 0:THW], E4S[0:4, lc * 128:(lc + 1) * 128],
                                               rowsb[0:4, roff + tt * THW: roff + tt * THW + THW],
                                               start=True, stop=True), "t", 1)

                            @block.vector
                            def _(e, lc=lc, tt=tt, psb=psb, dstb=dstb, srcb=srcb):
                                p.wait(e, "t")
                                col = lc * TH + tt * THW
                                p.inc(e.tensor_tensor(dstb[:, col:col + THW], srcb[:, col:col + THW],
                                                      psb[:, 0:THW], ALU.mult), "v", 1)
                            bi += 1

                @block.vector
                def _(e):
                    p.wait(e, "t", "a", "v")
                    e.tensor_tensor(QG[:, 0:T], QG[:, 0:T], ECD[:, 0:T], ALU.mult)
                    p.inc(e.tensor_tensor(KG[:, 0:T], KG[:, 0:T], DTE[:, 0:T], ALU.mult), "v", 1)

                for tp in range(TH // 128):
                    @block.tensor
                    def _(e, tp=tp):
                        p.wait(e, "v", "a")
                        for lc in range(2):
                            last = e.transpose(PSB2[lc][:, 0:128], KG[:, lc * TH + tp * 128: lc * TH + tp * 128 + 128],
                                               IDENT[:])
                        p.inc(last, "t", 1)

                    @block.scalar
                    def _(e, tp=tp):
                        p.wait(e, "t")
                        e.activation(KPT[:, tp * 256: tp * 256 + 128], PSB2[0][:, 0:128], AF.Copy)
                        p.inc(e.activation(KPT[:, tp * 256 + 128: tp * 256 + 256], PSB2[1][:, 0:128], AF.Copy), "a", 1)

                if th == 0:
                    @block.vector
                    def _(e):
                        p.wait(e, "a")
                        p.inc(e.memset(S4F[:], 0.0), "v", 1)

                    @block.scalar
                    def _(e):
                        p.wait(e, "v")
                        p.inc(e.activation(S4B[:], S4F[:], AF.Copy), "a", 1)

                t0 = p.cnt["t"]; v0 = p.cnt["v"]; a0 = p.cnt["a"]
                for n in range(NCHH):
                    tp = n // 2
                    for jp in range(2):
                        i = n * 2 + jp
                        lc = jp
                        bAK = PS[i % 2]
                        bOa = PS[2 + i % 2]
                        bOb = PS[4 + i % 2]
                        col = lc * TH + C * n

                        @block.tensor
                        def _(e, n=n, jp=jp, i=i, lc=lc, bAK=bAK, bOa=bOa, bOb=bOb, col=col, tp=tp,
                              t0=t0, v0=v0, a0=a0, g=g, th=th):
                            nb = 64 * (n % 2)
                            if i >= 2:
                                e.wait_ge(p.hs["v"], v0 + 3 * (i - 2) + 2)
                            else:
                                e.wait_ge(p.hs["v"], v0)
                                e.wait_ge(p.hs["a"], a0)
                            if i >= 1:
                                e.wait_ge(p.hs["t"], t0 + 3 * i)
                            first = e.matmul(bAK[nb:nb + 64, 0:C],
                                             KA[0:64, col:col + C],
                                             QA[0:64, col:col + C],
                                             start=True, stop=True, tile_position=(0, nb))
                            p.inc(first, "q", 1)
                            e.wait_ge(p.hs["q"], p.cnt["q"])
                            last = e.matmul(bAK[nb:nb + 64, C:2 * C],
                                            KA[64:128, col:col + C],
                                            QA[64:128, col:col + C],
                                            start=True, stop=True, tile_position=(64, nb))
                            p.inc(last, "t", 1)
                            if i >= 2:
                                e.wait_ge(p.hs["a"], a0 + 2 * (i - 2) + 2)
                                e.wait_ge(p.hs["v"], v0 + 3 * (i - 2) + 3)
                            e.wait_ge(p.hs["v"], v0 + 3 * i + 1)
                            for jj in range(2):
                                r0 = 64 * jj
                                e.matmul(bOa[r0:r0 + 64, 0:C],
                                         VT[nb:nb + 64, tp * 256 + (2 * jp + jj) * 64: tp * 256 + (2 * jp + jj) * 64 + 64],
                                         AM[i % 2][nb:nb + 64, jj * C:(jj + 1) * C],
                                         start=True, stop=True, tile_position=(nb, r0))
                            for jj in range(2):
                                r0 = 64 * jj
                                last = e.matmul(bOb[r0:r0 + 64, 0:C],
                                                S4B[r0:r0 + 64, 64 * jp:64 * jp + 64],
                                                QG[r0:r0 + 64, col:col + C],
                                                start=True, stop=True, tile_position=(r0, r0))
                            p.inc(last, "t", 1)
                            for jj in range(2):
                                r0 = 64 * jj
                                last = e.matmul(bAK[r0:r0 + 64, 256:256 + C],
                                                KPT[nb:nb + 64, tp * 256 + lc * 128 + 64 * jj: tp * 256 + lc * 128 + 64 * jj + 64],
                                                VT[nb:nb + 64, tp * 256 + (2 * jp + jj) * 64: tp * 256 + (2 * jp + jj) * 64 + 64],
                                                start=True, stop=True, tile_position=(nb, r0))
                            p.inc(last, "t", 1)

                        @block.vector
                        def _(e, n=n, jp=jp, i=i, bAK=bAK, bOa=bOa, t0=t0, v0=v0, a0=a0, g=g, th=th):
                            e.wait_ge(p.hs["t"], t0 + 3 * i + 1)
                            nb = 64 * (n % 2)
                            p.inc(e.tensor_tensor(AM[i % 2][nb:nb + 64, 0:2 * C], bAK[nb:nb + 64, 0:2 * C],
                                                  TRIU2[nb:nb + 64, 0:2 * C], ALU.mult), "v", 1)
                            e.wait_ge(p.hs["t"], t0 + 3 * i + 3)
                            p.inc(e.scalar_tensor_tensor(S4F[:, 64 * jp:64 * jp + 64],
                                                         S4F[:, 64 * jp:64 * jp + 64],
                                                         CDF[:, jp * NCHH + n: jp * NCHH + n + 1],
                                                         bAK[:, 256:256 + C], ALU.mult, ALU.add), "v", 1)
                            e.wait_ge(p.hs["a"], a0 + 2 * i + 1)
                            ocol = (2 * g + jp) * T + th * TH + C * n
                            p.inc(e.tensor_tensor(OF[:, ocol:ocol + C], bOa[:, 0:C],
                                                  OSCR[:, 0:C] if i % 2 == 0 else OSCR[:, C:2 * C],
                                                  ALU.add), "v", 1)

                        @block.scalar
                        def _(e, n=n, jp=jp, i=i, bOb=bOb, t0=t0, v0=v0, a0=a0):
                            e.wait_ge(p.hs["t"], t0 + 3 * i + 2)
                            if i >= 2:
                                e.wait_ge(p.hs["v"], v0 + 3 * (i - 2) + 3)
                            dst = OSCR[:, 0:C] if i % 2 == 0 else OSCR[:, C:2 * C]
                            p.inc(e.activation(dst, bOb[:, 0:C], AF.Copy), "a", 1)
                            e.wait_ge(p.hs["v"], v0 + 3 * i + 2)
                            p.inc(e.activation(S4B[:, 64 * jp:64 * jp + 64], S4F[:, 64 * jp:64 * jp + 64], AF.Copy), "a", 1)

    if upto <= 3:
        _finish()
        return nc
    mark("ph6_wo")
    WCH = sb("WCHb", [128, NCH * D], BF16)
    WFC = [sb(f"WFC{i}", [128, 2 * D], BF16) for i in range(2)]
    Y1B = BIG[2]
    load_wchunks(Wo0, WCH)
    for oc in range(NCH):
        for tt in range(NT):
            i = oc * NT + tt
            ps = PS[i % 2]

            @block.tensor
            def _(e, oc=oc, tt=tt, ps=ps, i=i):
                if i < 2:
                    p.wait(e, "dG", "a", "v")
                else:
                    e.wait_ge(p.hs["v"], p.cnt["v"] - 1)
                for hc in range(NCH):
                    last = e.matmul(ps[:, 0:TW], WCH[:, hc * D + oc * 128: hc * D + oc * 128 + 128],
                                    OF[:, hc * T + tt * TW: hc * T + tt * TW + TW],
                                    start=(hc == 0), stop=(hc == NCH - 1))
                p.inc(last, "t", 1)

            @block.vector
            def _(e, oc=oc, tt=tt, ps=ps):
                p.wait(e, "t", "dS")
                p.inc(e.tensor_tensor(FT[1][:, tt * TW:tt * TW + TW],
                                      ps[:, 0:TW], Y0B[:, oc * T + tt * TW: oc * T + tt * TW + TW],
                                      ALU.add), "v", 1)

            @block.scalar
            def _(e, oc=oc, tt=tt):
                p.wait(e, "v")
                p.inc(e.activation(Y1B[:, oc * T + tt * TW: oc * T + tt * TW + TW],
                                   FT[1][:, tt * TW:tt * TW + TW], AF.Copy), "a", 1)

        @block.sync
        def _(e, oc=oc):
            p.wait(e, "v", "a")
            p.inc(e.dma_start(out=y1d.ap()[oc * 128:(oc + 1) * 128, :], in_=FT[1][:, 0:T]), "dS", 16)

    if upto <= 4:
        _finish()
        return nc
    mark("ph78_ln2")
    ln_stats(Y1B, STAT[0], STAT[1])
    XN2 = BIG[0]
    ln_apply(None, Y1B, STAT[0], STAT[1], XN2, w="ln2w", b="ln2b")

    mark("ph9_ffn")
    GBUF = BIG[1]
    NFC = FFN // 128
    for tt in range(NT):
        for fc in range(NFC):
            ps = PS[fc % 2]

            if fc % 2 == 0:
                @block.gpsimd
                def _(e, tt=tt, fc=fc):
                    if fc >= 2 or tt > 0:
                        e.wait_ge(p.hs["t"], p.cnt["t"] - 1)
                    else:
                        p.wait(e, "v", "a")
                    p.inc(e.dma_start(out=WFC[(fc // 2) % 2][:, 0:2 * D].rearrange("p (c f) -> p c f", f=256),
                                      in_=W1.ap()[0:D, fc * 128:(fc + 2) * 128].rearrange("(c p) f -> p c f", p=128)), "dG", 16)

            @block.tensor
            def _(e, tt=tt, fc=fc, ps=ps):
                p.wait(e, "dG")
                if fc >= 2 or tt > 0:
                    e.wait_ge(p.hs["a"], p.cnt["a"] - 1)
                for dc in range(NCH):
                    last = e.matmul(ps[:, 0:TW], WFC[(fc // 2) % 2][:, dc * 256 + (fc % 2) * 128: dc * 256 + (fc % 2) * 128 + 128],
                                    XN2[:, dc * T + tt * TW: dc * T + tt * TW + TW],
                                    start=(dc == 0), stop=(dc == NCH - 1))
                p.inc(last, "t", 1)

            @block.scalar
            def _(e, tt=tt, fc=fc, ps=ps):
                p.wait(e, "t")
                e.activation(FT[0][:, 0:TW], ps[:, 0:TW], AF.Identity, bias=VECS[:, VOFF["b1"] + fc:VOFF["b1"] + fc + 1])
                p.inc(e.activation(GBUF[:, fc * TW:(fc + 1) * TW], FT[0][:, 0:TW], AF.Gelu), "a", 1)

        for half in range(2):
            for fc in range(NFC):
                if fc % 4 == 0:
                    @block.gpsimd
                    def _(e, fc=fc, half=half):
                        e.wait_ge(p.hs["t"], p.cnt["t"] - 1)
                        p.inc(e.dma_start(out=WFC[(fc // 4) % 2][:, 0:2048].rearrange("p (c f) -> p c f", f=512),
                                          in_=W2.ap()[fc * 128:(fc + 4) * 128, half * 512:(half + 1) * 512].rearrange("(c p) f -> p c f", p=128)), "dG", 16)

                @block.tensor
                def _(e, fc=fc, half=half, tt=tt):
                    p.wait(e, "dG")
                    if fc == 0:
                        p.wait(e, "v", "a")
                    for oc4 in range(4):
                        last = e.matmul(PS[2 + oc4][:, 0:TW],
                                        WFC[(fc // 4) % 2][:, (fc % 4) * 512 + oc4 * 128: (fc % 4) * 512 + (oc4 + 1) * 128],
                                        GBUF[:, fc * TW:(fc + 1) * TW],
                                        start=(fc == 0), stop=(fc == NFC - 1))
                    p.inc(last, "t", 1)

            for oc4 in range(4):
                oc = half * 4 + oc4

                @block.sync
                def _(e, oc=oc, tt=tt):
                    p.wait(e, "v", "a")
                    p.inc(e.dma_start(out=FT[0][:, 0:TW], in_=xT.ap()[oc * 128:(oc + 1) * 128, tt * TW:(tt + 1) * TW]), "dS", 16)
                    p.inc(e.dma_start(out=FT2[:, 0:TW], in_=y1d.ap()[oc * 128:(oc + 1) * 128, tt * TW:(tt + 1) * TW]), "dS", 16)

                @block.scalar
                def _(e, oc=oc, oc4=oc4, tt=tt):
                    p.wait(e, "t", "dS")
                    p.inc(e.activation(FT[1][:, 0:TW], PS[2 + oc4][:, 0:TW], AF.Identity,
                                       bias=vec("b2", oc)), "a", 1)

                @block.vector
                def _(e, oc=oc, tt=tt):
                    p.wait(e, "a")
                    e.tensor_tensor(FT[1][:, 0:TW], FT[1][:, 0:TW], FT[0][:, 0:TW], ALU.add)
                    p.inc(e.tensor_tensor(FT[1][:, 0:TW], FT[1][:, 0:TW], FT2[:, 0:TW], ALU.add), "v", 1)

                @block.sync
                def _(e, oc=oc, tt=tt):
                    p.wait(e, "v")
                    p.inc(e.dma_start(out=x1d.ap()[oc * 128:(oc + 1) * 128, tt * TW:(tt + 1) * TW],
                                      in_=FT[1][:, 0:TW]), "dS", 16)

    if upto <= 5:
        _finish()
        return nc

    mark("ph10_12_rwkv")
    X1B = BIG[2]
    cast_from_dram(x1d, X1B)
    ln_stats(X1B, STAT[2], STAT[3])
    YS = BIG[1]

    @block.vector
    def _(e):
        p.wait(e, "a", "t")
        for cc in range(NCH):
            last = e.memset(YS[:, cc * (T + 1):cc * (T + 1) + 1], 0.0)
        p.inc(last, "v", 1)

    ln_apply(x1d, None, STAT[2], STAT[3], YS, stride=T + 1, off0=1)

    XR = BIG[0]
    XK = BIG[2]

    @block.vector
    def _(e):
        p.wait(e, "v", "a")
        for cc in range(NCH):
            ycur = YS[:, cc * (T + 1) + 1: cc * (T + 1) + 1 + T]
            yprv = YS[:, cc * (T + 1): cc * (T + 1) + T]
            e.tensor_scalar(BSCR[:, 0:T], ycur, vec("mur", cc), None, ALU.mult)
            e.scalar_tensor_tensor(XR[:, cc * T:(cc + 1) * T], yprv, vec("omur", cc),
                                   BSCR[:, 0:T], ALU.mult, ALU.add)
            e.tensor_scalar(BSCR[:, 0:T], ycur, vec("muk", cc), None, ALU.mult)
            last = e.scalar_tensor_tensor(XK[:, cc * T:(cc + 1) * T], yprv, vec("omuk", cc),
                                          BSCR[:, 0:T], ALU.mult, ALU.add)
        p.inc(last, "v", 1)

    RB_ = BIG[1]
    load_wchunks(WrT, WCH)

    def evR(e, oc, tt, ps):
        p.inc(e.activation(RB_[:, oc * T + tt * TW: oc * T + tt * TW + TW], ps[:, 0:TW], AF.Sigmoid), "a", 1)

    proj_fmajor(WCH, D, lambda oc: oc * 128, XR, T, NCH, evR)

    KKB = BIG[0]
    load_wchunks(WkTr, WCH)

    def evKK(e, oc, tt, ps):
        e.activation(FT[0][:, 0:TW], ps[:, 0:TW], AF.Relu)
        p.inc(e.activation(KKB[:, oc * T + tt * TW: oc * T + tt * TW + TW], FT[0][:, 0:TW], AF.Square), "a", 1)

    proj_fmajor(WCH, D, lambda oc: oc * 128, XK, T, NCH, evKK)

    X2B = BIG[2]
    load_wchunks(WvTr, WCH)
    for oc in range(NCH):
        @block.sync
        def _(e, oc=oc):
            p.wait(e, "v", "a")
            p.inc(e.dma_start(out=FT[0][:, 0:T], in_=x1d.ap()[oc * 128:(oc + 1) * 128, :]), "dS", 16)

        for tt in range(NT):
            i = oc * NT + tt
            ps = PS[i % 2]

            @block.tensor
            def _(e, oc=oc, tt=tt, ps=ps, i=i):
                if i < 2:
                    p.wait(e, "dG", "a", "v")
                else:
                    e.wait_ge(p.hs["v"], p.cnt["v"] - 1)
                for dc in range(NCH):
                    last = e.matmul(ps[:, 0:TW], WCH[:, dc * D + oc * 128: dc * D + oc * 128 + 128],
                                    KKB[:, dc * T + tt * TW: dc * T + tt * TW + TW],
                                    start=(dc == 0), stop=(dc == NCH - 1))
                p.inc(last, "t", 1)

            @block.vector
            def _(e, oc=oc, tt=tt, ps=ps):
                p.wait(e, "t", "dS")
                e.tensor_tensor(FT[1][:, 0:TW], ps[:, 0:TW],
                                RB_[:, oc * T + tt * TW: oc * T + tt * TW + TW], ALU.mult)
                p.inc(e.tensor_tensor(FT[1][:, 0:TW], FT[1][:, 0:TW],
                                      FT[0][:, tt * TW:(tt + 1) * TW], ALU.add), "v", 1)

            @block.scalar
            def _(e, oc=oc, tt=tt):
                p.wait(e, "v")
                p.inc(e.activation(X2B[:, oc * T + tt * TW: oc * T + tt * TW + TW],
                                   FT[1][:, 0:TW], AF.Copy), "a", 1)

            @block.sync
            def _(e, oc=oc, tt=tt):
                p.wait(e, "v")
                p.inc(e.dma_start(out=x2d.ap()[oc * 128:(oc + 1) * 128, tt * TW:(tt + 1) * TW],
                                  in_=FT[1][:, 0:TW]), "dS", 16)

    if upto <= 6:
        _finish()
        return nc

    mark("ph13_14_l1proj")
    ln_stats(X2B, STAT[0], STAT[1])
    XA = BIG[0]
    ln_apply(x2d, None, STAT[0], STAT[1], XA)

    Q3 = BIG[1]
    load_wchunks(Wq1, WCH)

    def evQ3(e, oc, tt, ps):
        p.inc(e.activation(Q3[:, oc * T + tt * TW: oc * T + tt * TW + TW], ps[:, 0:TW], AF.Copy), "a", 1)

    proj_fmajor(WCH, D, lambda oc: oc * 128, XA, T, NCH, evQ3)

    K3 = BIG[2]
    load_wchunks(Wkv1, WCH)

    def evK3(e, oc, tt, ps):
        p.inc(e.activation(K3[:, oc * T + tt * TW: oc * T + tt * TW + TW], ps[:, 0:TW], AF.Copy), "a", 1)

    proj_fmajor(WCH, D, lambda oc: oc * 128, XA, T, NCH, evK3)

    with ExitStack() as lctx:
        def lsb(nm, shape, dt=BF16):
            return lctx.enter_context(nc.sbuf_tensor(nm, shape, dt))

        VSTG = [lsb(f"lVSTG{i}", [128, 512]) for i in range(2)]
        PBUF = [lsb(f"lPBUF{i}", [128, T]) for i in range(2)]
        PMB = [lsb(f"lPMB{i}", [128, TW]) for i in range(2)]
        GMX = lsb("lGMX", [128, T])
        VHH = lsb("lVHH", [128, NTT * 65])
        MQ = lctx.enter_context(nc.sbuf_tensor("lMQ", [128, NTT], F32))
        DQ = lctx.enter_context(nc.sbuf_tensor("lDQ", [128, NTT], F32))
        RQ = lctx.enter_context(nc.sbuf_tensor("lRQ", [128, NTT], F32))
        RQB = lsb("lRQB", [128, NTT])
        DENR = lctx.enter_context(nc.sbuf_tensor("lDENR", [1, T], F32))
        RROWB = lsb("lRROWB", [1, T])
        RBB = lsb("lRBB", [64, T])
        OUT3S = lsb("lOUT3S", [64, T])

        load_wchunks(Wkv1, WCH, cols=D, col_off=D)
        a0v = p.cnt["a"]; d0v = p.cnt["dS"]
        for tp in range(NTT):
            for vh in range(2):
                i = tp * 2 + vh
                ps = PS[i % 2]

                @block.tensor
                def _(e, tp=tp, vh=vh, ps=ps, i=i):
                    if i < 2:
                        p.wait(e, "dG", "v", "a")
                    else:
                        e.wait_ge(p.hs["a"], p.cnt["a"] - 1)
                    for dc in range(NCH):
                        last = e.matmul(ps[:, 0:512], XA[:, dc * T + tp * 128: dc * T + tp * 128 + 128],
                                        WCH[:, dc * D + vh * 512: dc * D + vh * 512 + 512],
                                        start=(dc == 0), stop=(dc == NCH - 1))
                    p.inc(last, "t", 1)

                @block.scalar
                def _(e, tp=tp, vh=vh, ps=ps, i=i):
                    p.wait(e, "t")
                    if i >= 2:
                        e.wait_ge(p.hs["dS"], d0v + 16 * (i - 1))
                    p.inc(e.activation(VSTG[i % 2][:, 0:512], ps[:, 0:512], AF.Copy), "a", 1)

                @block.sync
                def _(e, tp=tp, vh=vh, i=i):
                    e.wait_ge(p.hs["a"], a0v + i + 1)
                    p.inc(e.dma_start(out=v3d.ap()[tp * 128:(tp + 1) * 128, vh * 512:(vh + 1) * 512],
                                      in_=VSTG[i % 2][:, 0:512]), "dS", 16)

        if upto <= 7:
            _finish()
            return nc

        mark("ph15_16_attn")
        OUT3 = BIG[0]

        @block.vector
        def _(e):
            p.wait(e, "a")
            p.inc(e.memset(VHH.rearrange("p (n z) -> p n z", z=65)[:, :, 64:65], 1.0), "v", 1)

        for h in range(H):
            hr0 = 64 * (h % 2)
            hcol = (h // 2) * T

            @block.gpsimd
            def _(e, h=h):
                p.wait(e, "t", "dS")
                p.inc(e.dma_start(out=VHH.rearrange("p (n z) -> p n z", z=65)[:, :, 0:64],
                                  in_=v3d.ap()[:, h * 64:(h + 1) * 64].rearrange("(n p) f -> p n f", p=128)), "dG", 16)

            t0 = p.cnt["t"]; a0 = p.cnt["a"]; v0 = p.cnt["v"]
            KPT_ = TW // 128
            qb_b = [(kc * 128) // TW for kc in range(NTT)]
            n_pv = [NT - qb_b[kc] for kc in range(NTT)]
            pre_t = [0]
            for kc in range(NTT):
                pre_t.append(pre_t[-1] + NT + n_pv[kc])

            def t_of_s(kc, qb, t0=t0, pre_t=pre_t):
                return t0 + pre_t[kc] + qb + 1

            def t_of_pv(kc, qb, t0=t0, pre_t=pre_t, qb_b=qb_b):
                return t0 + pre_t[kc] + NT + (qb - qb_b[kc]) + 1

            kmax = [((qb + 1) * TW) // 128 - 1 for qb in range(NT)]

            @block.tensor
            def _(e, h=h, hr0=hr0, hcol=hcol, t0=t0, a0=a0, v0=v0, qb_b=qb_b, kmax=kmax,
                  t_of_s=t_of_s, t_of_pv=t_of_pv):
                p.wait(e, "a", "v", "dG")
                for kc in range(NTT):
                    for qb in range(NT):
                        si = kc * NT + qb
                        if si >= 2:
                            e.wait_ge(p.hs["a"], a0 + si - 1)
                        p.inc(e.matmul(PS[qb % 2][:, 0:TW],
                                       K3[hr0:hr0 + 64, hcol + kc * 128: hcol + kc * 128 + 128],
                                       Q3[hr0:hr0 + 64, hcol + qb * TW: hcol + qb * TW + TW],
                                       start=True, stop=True), "t", 1)
                    for qb in range(qb_b[kc], NT):
                        if qb == qb_b[kc]:
                            e.wait_ge(p.hs["v"], v0 + 2 * kc + 1)
                            src = PMB[kc % 2][:, 0:TW]
                        else:
                            e.wait_ge(p.hs["a"], a0 + kc * NT + qb + 1)
                            src = PBUF[kc % 2][:, qb * TW: qb * TW + TW]
                        p.inc(e.matmul(PS[2 + qb][0:65, 0:TW],
                                       VHH[:, kc * 65: kc * 65 + 65],
                                       src,
                                       start=(kc == 0), stop=(kc == kmax[qb])), "t", 1)

            @block.scalar
            def _(e, h=h, t0=t0, a0=a0, v0=v0, qb_b=qb_b, t_of_s=t_of_s, t_of_pv=t_of_pv):
                for kc in range(NTT):
                    for qb in range(NT):
                        e.wait_ge(p.hs["t"], t_of_s(kc, qb))
                        if kc >= 2:
                            if qb >= qb_b[kc - 2]:
                                e.wait_ge(p.hs["t"], t_of_pv(kc - 2, qb))
                            e.wait_ge(p.hs["v"], v0 + 2 * (kc - 2) + 2)
                        p.inc(e.activation(PBUF[kc % 2][:, qb * TW: qb * TW + TW],
                                           PS[qb % 2][:, 0:TW], AF.Exp, scale=0.125), "a", 1)

            @block.vector
            def _(e, h=h, t0=t0, a0=a0, v0=v0, qb_b=qb_b, t_of_pv=t_of_pv):
                for kc in range(NTT):
                    e.wait_ge(p.hs["a"], a0 + kc * NT + qb_b[kc] + 1)
                    if kc >= 2:
                        e.wait_ge(p.hs["t"], t_of_pv(kc - 2, qb_b[kc - 2]))
                    dd = kc % (TW // 128)
                    p.inc(e.tensor_tensor(PMB[kc % 2][:, 0:TW],
                                          PBUF[kc % 2][:, qb_b[kc] * TW: qb_b[kc] * TW + TW],
                                          CMSM[:, 384 - 128 * dd: 384 - 128 * dd + TW], ALU.mult), "v", 1)
                    e.wait_ge(p.hs["a"], a0 + kc * NT + NT)
                    if kc == 0:
                        p.inc(e.tensor_copy(GMX[:, 0:T], PBUF[0][:, 0:T]), "v", 1)
                    else:
                        p.inc(e.tensor_tensor(GMX[:, 0:T], GMX[:, 0:T], PBUF[kc % 2][:, 0:T], ALU.max), "v", 1)

            tf0 = p.cnt["t"]; vf0 = p.cnt["v"]
            @block.tensor
            def _(e, h=h, tf0=tf0, vf0=vf0):
                p.wait(e, "v")
                for qt in range(NTT):
                    if qt >= 2:
                        e.wait_ge(p.hs["v"], vf0 + qt - 1)
                    p.inc(e.transpose(PSB2[qt % 2][:, 0:128],
                                      GMX[:, qt * 128: qt * 128 + 128], IDENT[:]), "t", 1)

            @block.vector
            def _(e, h=h, tf0=tf0):
                for qt in range(NTT):
                    e.wait_ge(p.hs["t"], tf0 + qt + 1)
                    p.inc(e.reduce_max(MQ[:, qt:qt + 1], PSB2[qt % 2][:, 0:128], axis=AX.X), "v", 1)

            @block.scalar
            def _(e, h=h):
                p.wait(e, "t", "v")
                for qb in range(NT):
                    last = e.activation(FT[0][64:65, qb * TW: qb * TW + TW], PS[2 + qb][64:65, 0:TW], AF.Copy)
                p.inc(last, "a", 1)

            @block.sync
            def _(e, h=h):
                p.wait(e, "a")
                p.inc(e.dma_start(out=DENR[0:1, 0:T], in_=FT[0][64:65, 0:T]), "dS", 16)

            td0 = p.cnt["t"]; vd0 = p.cnt["v"]
            @block.tensor
            def _(e, h=h, td0=td0, vd0=vd0):
                p.wait(e, "a", "dS")
                for qt in range(NTT):
                    if qt >= 2:
                        e.wait_ge(p.hs["v"], vd0 + qt - 1)
                    p.inc(e.transpose(PS[qt % 2][:, 0:1], DENR[0:1, qt * 128: qt * 128 + 128],
                                      IDENTF[0:1, 0:1]), "t", 1)

            @block.vector
            def _(e, h=h, td0=td0):
                for qt in range(NTT):
                    e.wait_ge(p.hs["t"], td0 + qt + 1)
                    p.inc(e.tensor_copy(DQ[:, qt:qt + 1], PS[qt % 2][:, 0:1]), "v", 1)
                e.wait_ge(p.hs["v"], p.cnt["v"])

            @block.vector
            def _(e, h=h):
                p.wait(e, "t")
                p.chain(e, e.scalar_tensor_tensor(RQ[:, 0:NTT], MQ[:, 0:NTT], 1e-8, DQ[:, 0:NTT], ALU.mult, ALU.add))
                p.inc(e.reciprocal(RQ[:, 0:NTT], RQ[:, 0:NTT]), "v", 1)

            @block.scalar
            def _(e, h=h):
                p.wait(e, "v")
                p.inc(e.activation(RQB[:, 0:NTT], RQ[:, 0:NTT], AF.Copy), "a", 1)

            tr0 = p.cnt["t"]; ar0 = p.cnt["a"]
            @block.tensor
            def _(e, h=h, tr0=tr0, ar0=ar0):
                p.wait(e, "a")
                for qt in range(NTT):
                    if qt >= 2:
                        e.wait_ge(p.hs["a"], ar0 + qt - 1)
                    p.inc(e.transpose(PSB2[qt % 2][0:1, 0:128], RQB[:, qt:qt + 1], IDENT[:]), "t", 1)

            @block.scalar
            def _(e, h=h, tr0=tr0):
                for qt in range(NTT):
                    e.wait_ge(p.hs["t"], tr0 + qt + 1)
                    p.inc(e.activation(RROWB[0:1, qt * 128: qt * 128 + 128], PSB2[qt % 2][0:1, 0:128], AF.Copy), "a", 1)

            tb0 = p.cnt["t"]; ab0 = p.cnt["a"]
            @block.tensor
            def _(e, h=h, tb0=tb0, ab0=ab0):
                p.wait(e, "a")
                for qb in range(NT):
                    if qb >= 2:
                        e.wait_ge(p.hs["a"], ab0 + qb - 1)
                    p.inc(e.matmul(PS[qb % 2][0:64, 0:TW], ONECOL[0:1, 0:64],
                                   RROWB[0:1, qb * TW: qb * TW + TW], start=True, stop=True), "t", 1)

            @block.scalar
            def _(e, h=h, tb0=tb0):
                for qb in range(NT):
                    e.wait_ge(p.hs["t"], tb0 + qb + 1)
                    p.inc(e.activation(RBB[0:64, qb * TW: qb * TW + TW], PS[qb % 2][0:64, 0:TW], AF.Copy), "a", 1)

            @block.vector
            def _(e, h=h):
                p.wait(e, "a", "dS")
                for qb in range(NT):
                    last = e.tensor_tensor(OUT3S[0:64, qb * TW: qb * TW + TW],
                                           PS[2 + qb][0:64, 0:TW], RBB[0:64, qb * TW: qb * TW + TW],
                                           ALU.mult)
                p.inc(last, "v", 1)

            @block.sync
            def _(e, h=h, hr0=hr0, hcol=hcol):
                p.wait(e, "v")
                p.inc(e.dma_start(out=OUT3[hr0:hr0 + 64, hcol: hcol + T], in_=OUT3S[0:64, 0:T]), "dS", 16)

    if upto <= 8:
        _finish()
        return nc
    mark("ph17_wo1")
    X3B = BIG[1]
    load_wchunks(Wo1, WCH)
    for oc in range(NCH):
        @block.sync
        def _(e, oc=oc):
            p.wait(e, "v", "a")
            p.inc(e.dma_start(out=FT[0][:, 0:T], in_=x2d.ap()[oc * 128:(oc + 1) * 128, :]), "dS", 16)

        for tt in range(NT):
            i = oc * NT + tt
            ps = PS[i % 2]

            @block.tensor
            def _(e, oc=oc, tt=tt, ps=ps, i=i):
                if i < 2:
                    p.wait(e, "dG", "v", "a")
                else:
                    e.wait_ge(p.hs["v"], p.cnt["v"] - 1)
                for hc in range(NCH):
                    last = e.matmul(ps[:, 0:TW], WCH[:, hc * D + oc * 128: hc * D + oc * 128 + 128],
                                    OUT3[:, hc * T + tt * TW: hc * T + tt * TW + TW],
                                    start=(hc == 0), stop=(hc == NCH - 1))
                p.inc(last, "t", 1)

            @block.scalar
            def _(e, oc=oc, tt=tt, ps=ps):
                p.wait(e, "t", "dS")
                p.inc(e.activation(FT[1][:, tt * TW: tt * TW + TW], ps[:, 0:TW], AF.Identity,
                                   bias=vec("bo1", oc)), "a", 1)

            @block.vector
            def _(e, oc=oc, tt=tt):
                p.wait(e, "a", "dS")
                p.inc(e.tensor_tensor(FT[1][:, tt * TW: tt * TW + TW],
                                      FT[1][:, tt * TW: tt * TW + TW],
                                      FT[0][:, tt * TW: tt * TW + TW], ALU.add), "v", 1)

            @block.scalar
            def _(e, oc=oc, tt=tt):
                p.wait(e, "v")
                p.inc(e.activation(X3B[:, oc * T + tt * TW: oc * T + tt * TW + TW],
                                   FT[1][:, tt * TW: tt * TW + TW], AF.Copy), "a", 1)

        @block.sync
        def _(e, oc=oc):
            p.wait(e, "v")
            p.inc(e.dma_start(out=x3T.ap()[oc * 128:(oc + 1) * 128, :], in_=FT[1][:, 0:T]), "dS", 16)

    mark("ph18_yln")
    ln_stats(X3B, STAT[0], STAT[1])
    YLNB = BIG[2]
    ln_apply(None, X3B, STAT[0], STAT[1], YLNB)

    for cc in range(NCH):
        @block.sync
        def _(e, cc=cc):
            p.wait(e, "v")
            p.inc(e.dma_start(out=ylnT.ap()[cc * 128:(cc + 1) * 128, :],
                              in_=YLNB[:, cc * T:(cc + 1) * T]), "dS", 16)

    GWB = sb("GWBb", [128, 64], BF16)
    SCR8 = sb("SCR8f", [8, T], F32)

    @block.gpsimd
    def _(e):
        p.wait(e, "v", "t")
        for dc in range(NCH):
            p.inc(e.dma_start(out=GWB[:, dc * 8:(dc + 1) * 8],
                              in_=gate_wT.ap()[dc * 128:(dc + 1) * 128, :]), "dG", 16)

    for tt in range(NT):
        ps = PS[tt % 2]

        @block.tensor
        def _(e, tt=tt, ps=ps):
            if tt < 2:
                p.wait(e, "dG", "v", "a")
            else:
                e.wait_ge(p.hs["a"], p.cnt["a"] - 1)
            for dc in range(NCH):
                last = e.matmul(ps[0:8, 0:TW], GWB[:, dc * 8:(dc + 1) * 8],
                                YLNB[:, dc * T + tt * TW: dc * T + tt * TW + TW],
                                start=(dc == 0), stop=(dc == NCH - 1))
            p.inc(last, "t", 1)

        @block.scalar
        def _(e, tt=tt, ps=ps):
            p.wait(e, "t")
            p.inc(e.activation(SCR8[0:8, tt * TW:(tt + 1) * TW], ps[0:8, 0:TW], AF.Copy), "a", 1)

    @block.sync
    def _(e):
        p.wait(e, "a")
        p.inc(e.dma_start(out=scoresT.ap()[:], in_=SCR8[0:8, 0:T]), "dS", 16)

    mark(None)
    @block.sync
    def _(e):
        p.wait(e, "dS", "dG", "dA")

    @block.vector
    def _(e):
        p.wait(e, "t", "a", "g")

    return nc


from concourse.bass_core_rust import IndirectOffsetOnAxis
I32 = mybir.dt.int32
NFC = FFN // 128
G0 = 0.7978845608028654
G1 = 0.044715
NE = 8


def build_launch2(NTOK=16384, K=2048, NITER=34):
    nc = bass.Bass("TRN2", target_bir_lowering=False, detect_race_conditions=False)
    NP2 = NTOK // 128
    KT = K // 128
    TWK = min(512, K)
    NTK = K // TWK
    QK = TWK // 128

    def ein(name, shape, dt=F32):
        return nc.dram_tensor(name, list(shape), dt, kind="ExternalInput")

    yln = ein("yln", (NTOK, D), BF16)
    sc_pm = ein("sc_pm", (NP2, 128))
    fc1T = ein("fc1T", (D, FFN))
    fc2T = ein("fc2T", (FFN, D))
    b1c = ein("b1c", (128, NFC))
    b2c = ein("b2c", (128, NCH))
    gamma_r = ein("gamma_r", (1, D))
    beta_r = ein("beta_r", (1, D))

    plane = nc.dram_tensor("plane", [NTOK, D], BF16, kind="ExternalOutput")
    kw_cm_d = nc.dram_tensor("kw_cm", [128, NP2], F32, kind="ExternalOutput")

    ones_c = nc.inline_tensor(np.ones((NP2, NP2)).astype(BF), name="ones_b16")
    onecol_c = nc.inline_tensor(np.ones((1, 128)).astype(BF), name="onecol_b16")
    onecolf_c = nc.inline_tensor(np.ones((1, 128)).astype(np.float32), name="onecol_f32")
    ust_c = nc.inline_tensor(np.triu(np.ones((NP2, NP2)), 1).astype(BF), name="ust_b16")
    identf_c = nc.inline_tensor(np.eye(128).astype(np.float32), name="id_f32")
    ident_c = nc.inline_tensor(np.eye(128).astype(BF), name="id_b16")
    iotar_c = nc.inline_tensor(np.tile(np.arange(128, dtype=np.float32), (128, 1)), name="iotar_f32")
    iotac_c = nc.inline_tensor(np.arange(128, dtype=np.float32).reshape(128, 1), name="iotac_f32")
    tidm_c = nc.inline_tensor(
        (np.arange(128, dtype=np.float32).reshape(128, 1) + 128.0 * np.arange(NP2, dtype=np.float32)[None, :]).copy(),
        name="tidm_f32")

    ctx = ExitStack()
    block = ctx.enter_context(nc.Block())
    p = Prog(nc, block, ctx)

    def sb(name, shape, dt=BF16):
        return ctx.enter_context(nc.sbuf_tensor(name, shape, dt))

    def f32sb(name, shape):
        return ctx.enter_context(nc.sbuf_tensor(name, shape, F32))

    CTM = sb("CTMb", [128, KT * D])
    YCT = sb("YCTb", [128, NCH * K])
    GB2 = sb("GB2b", [128, NFC * TWK])
    OTM = sb("OTMb", [128, QK * D])
    WF1C = [sb(f"WF1C{i}", [128, 2 * D]) for i in range(2)]
    WF2C = [sb(f"WF2C{i}", [128, 2048]) for i in range(2)]
    GAMB = sb("GAMBb", [128, D])
    BETB = sb("BETBb", [128, D])
    ZTILE = sb("ZTILEb", [128, D])
    FFT = [f32sb(f"FFT{i}", [128, TWK]) for i in range(2)]
    UU = [f32sb(f"UU{i}", [128, TWK]) for i in range(2)]
    FB = [sb(f"FB{i}", [128, TWK]) for i in range(2)]
    ONES = sb("ONESb", [NP2, NP2])
    ONECOL = sb("ONECOLb", [1, 128])
    ONECOLF = f32sb("ONECOLFf", [1, 128])
    UST = sb("USTb", [NP2, NP2])
    IDENTF = f32sb("IDENTff", [128, 128])
    IDENT = sb("IDENTb", [128, 128])
    IOTAR = f32sb("IOTARf", [128, 128])
    IOTAC = f32sb("IOTACf", [128, 1])
    GAMR = sb("GAMRb", [1, D])
    BETR = sb("BETRb", [1, D])
    B1S = f32sb("B1Sf", [128, NFC])
    B2S = f32sb("B2Sf", [128, NCH])

    SC = f32sb("SCf", [NP2, 128])
    KW = f32sb("KWf", [NP2, 128])
    KEPT = f32sb("KEPTf", [NP2, 128])
    CB = sb("CBb", [NP2, 128])
    PFX = f32sb("PFXf", [NP2, 128])
    EXCM = f32sb("EXCMf", [NP2, 128])
    NK = f32sb("NKf", [NP2, 128])
    OCOL = f32sb("OCOLf", [NP2, 1])
    RCB = sb("RCBb", [NP2, 1])
    LO = f32sb("LOf", [NP2, 1])
    HI = f32sb("HIf", [NP2, 1])
    MID = f32sb("MIDf", [NP2, 1])
    TOTC = f32sb("TOTCf", [NP2, 1])
    PRED = f32sb("PREDf", [NP2, 1])
    DLT = f32sb("DLTf", [NP2, 1])
    ONE1 = f32sb("ONE1f", [NP2, 1])
    KWCM = f32sb("KWCMf", [128, NP2])
    EXCM_CM = f32sb("EXCMCMf", [128, NP2])
    TOSLM = f32sb("TOSLMf", [128, NP2])
    PTF8 = f32sb("PTF8f", [128, 8 * 128])
    TIDM = f32sb("TIDMf", [128, NP2])
    OREP = f32sb("OREPf", [128, NP2])
    OROW = f32sb("OROWf", [1, 128])
    SCOL = f32sb("SCOLf", [128, 1])
    INDS = f32sb("INDSf", [128, NP2])
    PCOL = f32sb("PCOLf", [128, 1])
    OOFP = f32sb("OOFPf", [128, 1])
    KCOL = f32sb("KCOLf", [128, 1])
    AS_ = f32sb("ASf", [128, NP2])
    KROWS = f32sb("KROWSf", [1, 128])
    BST = f32sb("BSTf", [128, 128])
    CSC = f32sb("CSCf", [128, NP2])
    TOSF = f32sb("TOSFf", [128, KT])
    TOSI = ctx.enter_context(nc.sbuf_tensor("TOSIi", [128, KT], I32))
    MEANS = f32sb("MEANSf", [128, KT])
    RSTDS = f32sb("RSTDSf", [128, KT])
    SQF = f32sb("SQFf", [128, D])

    PS = [ctx.enter_context(nc.psum_tensor(f"PS{i}", [128, 512], F32)) for i in range(6)]
    PSB2 = [ctx.enter_context(nc.psum_tensor(f"PSB2_{i}", [128, 512], BF16)) for i in range(2)]

    _sc = {"cur": None}

    def mark(name):
        if _sc["cur"] is not None:
            nc.leave_named_scope(_sc["cur"][0], _sc["cur"][1], False)
            _sc["cur"] = None
        if name:
            sid, _ = nc.enter_named_scope(name, False)
            _sc["cur"] = (name, sid)

    @block.sync
    def _(e):
        for dst, src in [(ONES[:], ones_c), (ONECOL[:], onecol_c), (ONECOLF[:], onecolf_c),
                         (UST[:], ust_c), (IDENTF[:], identf_c), (IDENT[:], ident_c),
                         (IOTAR[:], iotar_c), (IOTAC[:], iotac_c)]:
            p.inc(e.dma_start(out=dst, in_=src.ap()), "dS", 16)
        p.inc(e.dma_start(out=SC[:], in_=sc_pm.ap()), "dS", 16)
        p.inc(e.dma_start(out=TIDM[:], in_=tidm_c.ap()), "dS", 16)
        p.inc(e.dma_start(out=B1S[:], in_=b1c.ap()), "dS", 16)
        p.inc(e.dma_start(out=B2S[:], in_=b2c.ap()), "dS", 16)

    @block.vector
    def _(e):
        p.inc(e.memset(ZTILE[:], 0.0), "v", 1)

    @block.scalar
    def _(e):
        p.wait(e, "v")
        for j in range(NP2):
            p.inc(e.dma_start(out=plane.ap()[j * 128:(j + 1) * 128, :], in_=ZTILE[:]), "dS", 16)

    @block.gpsimd
    def _(e):
        p.inc(e.dma_start(out=GAMR[:], in_=gamma_r.ap()), "dG", 16)
        p.inc(e.dma_start(out=BETR[:], in_=beta_r.ap()), "dG", 16)

    for gi, (row, dst) in enumerate([(GAMR, GAMB), (BETR, BETB)]):
        @block.tensor
        def _(e, gi=gi, row=row):
            if gi == 0:
                p.wait(e, "dG", "dS")
            else:
                p.wait(e, "a")
            for hh in range(2):
                p.inc(e.matmul(PS[hh][:, 0:512], ONECOL[0:1, 0:128],
                               row[0:1, hh * 512:(hh + 1) * 512], start=True, stop=True), "t", 1)

        @block.scalar
        def _(e, dst=dst):
            p.wait(e, "t")
            for hh in range(2):
                last = e.activation(dst[:, hh * 512:(hh + 1) * 512], PS[hh][:, 0:512], AF.Copy)
            p.inc(last, "a", 1)

    mark("p1_bisect")
    @block.vector
    def _(e):
        p.wait(e, "dS", "a")
        p.chain(e, e.memset(LO[:], -64.0))
        p.chain(e, e.memset(HI[:], 64.0))
        p.chain(e, e.memset(ONE1[:], 1.0))

    for it in range(NITER):
        @block.vector
        def _(e, it=it):
            p.chain(e, e.tensor_tensor(MID[:], LO[:], HI[:], ALU.add))
            p.chain(e, e.tensor_scalar_mul(MID[:], MID[:], 0.5))
            with nc.allow_low_precision(reason="0/1 mask"):
                p.inc(e.tensor_scalar(CB[:], SC[:], MID[:, 0:1], None, ALU.is_ge), "v", 1)

        @block.tensor
        def _(e, it=it):
            p.wait(e, "v")
            p.inc(e.matmul(PS[0][0:NP2, 0:128], ONES[:], CB[:], start=True, stop=True), "t", 1)

        @block.vector
        def _(e, it=it):
            p.wait(e, "t")
            p.chain(e, e.reduce_sum(TOTC[:], PS[0][0:NP2, 0:128], axis=AX.X))
            p.chain(e, e.tensor_scalar(PRED[:], TOTC[:], float(K), None, ALU.is_ge))
            p.chain(e, e.tensor_tensor(DLT[:], MID[:], LO[:], ALU.subtract))
            p.chain(e, e.scalar_tensor_tensor(LO[:], DLT[:], 1.0, PRED[:], ALU.mult, ALU.mult_add_to_dst)
                    if False else e.tensor_tensor(PRED[:], PRED[:], DLT[:], ALU.mult))
            p.chain(e, e.tensor_tensor(LO[:], LO[:], PRED[:], ALU.add))
            p.chain(e, e.tensor_tensor(HI[:], LO[:], DLT[:], ALU.add))

    mark("p2_slots")
    @block.vector
    def _(e):
        p.chain(e, e.tensor_scalar(KEPT[:], SC[:], LO[:, 0:1], None, ALU.is_ge))

    @block.scalar
    def _(e):
        p.wait(e, "v", "q")
        p.inc(e.activation(KW[:], SC[:], AF.Tanh), "a", 1)

    @block.vector
    def _(e):
        p.wait(e, "a")
        p.chain(e, e.tensor_scalar(KW[:], KW[:], 0.5, 0.5, ALU.mult, ALU.add))
        p.chain(e, e.tensor_tensor(KW[:], KW[:], KEPT[:], ALU.mult))
        p.chain(e, e.tensor_tensor_scan(PFX[:], ONE1[:, 0:1].broadcast_to((NP2, 128)),
                                        KEPT[:], 0.0, ALU.mult, ALU.add))
        p.chain(e, e.tensor_tensor(EXCM[:], PFX[:], KEPT[:], ALU.subtract))
        p.chain(e, e.tensor_scalar(NK[:], KEPT[:], -1.0, 1.0, ALU.mult, ALU.add))
        p.chain(e, e.tensor_tensor(EXCM[:], EXCM[:], KEPT[:], ALU.mult))
        p.chain(e, e.scalar_tensor_tensor(EXCM[:], NK[:], 100000.0, EXCM[:], ALU.mult, ALU.add))
        with nc.allow_low_precision(reason="row counts <= 128 exact in bf16"):
            p.chain(e, e.tensor_copy(RCB[:], PFX[:, 127:128]))

    @block.tensor
    def _(e):
        p.wait(e, "v", "q")
        p.inc(e.matmul(PS[1][0:NP2, 0:1], UST[:], RCB[:], start=True, stop=True), "t", 1)

    @block.vector
    def _(e):
        p.wait(e, "t")
        p.chain(e, e.tensor_copy(OCOL[:], PS[1][0:NP2, 0:1]))

    tt2 = p.cnt["t"]
    @block.tensor
    def _(e):
        p.wait(e, "v", "q")
        p.inc(e.transpose(PS[0][0:128, 0:NP2], EXCM[:], IDENTF[0:NP2, 0:NP2]), "t", 1)
        p.inc(e.transpose(PS[1][0:128, 0:NP2], KW[:], IDENTF[0:NP2, 0:NP2]), "t", 1)
        p.inc(e.transpose(PS[2][0:1, 0:NP2], OCOL[:], IDENTF[0:NP2, 0:NP2]), "t", 1)

    @block.vector
    def _(e, tt2=tt2):
        e.wait_ge(p.hs["t"], tt2 + 3)
        p.chain(e, e.tensor_copy(EXCM_CM[:], PS[0][0:128, 0:NP2]))
        p.chain(e, e.tensor_copy(KWCM[:], PS[1][0:128, 0:NP2]))
        p.chain(e, e.tensor_copy(OROW[0:1, 0:NP2], PS[2][0:1, 0:NP2]))

    @block.sync
    def _(e):
        p.wait(e, "v", "q")
        p.inc(e.dma_start(out=kw_cm_d.ap(), in_=KWCM[:]), "dS", 16)

    tt3 = p.cnt["t"]
    @block.tensor
    def _(e):
        p.wait(e, "v", "q")
        p.inc(e.matmul(PS[3][:, 0:NP2], ONECOLF[0:1, 0:128], OROW[0:1, 0:NP2],
                       start=True, stop=True), "t", 1)

    @block.vector
    def _(e, tt3=tt3):
        e.wait_ge(p.hs["t"], tt3 + 1)
        p.chain(e, e.tensor_copy(OREP[:], PS[3][:, 0:NP2]))

    mark("p3_toslm")
    t30 = p.cnt["t"]
    for pg in range(0, NP2, 8):
        n8 = min(8, NP2 - pg)

        @block.vector
        def _(e, pg=pg, n8=n8, t30=t30):
            if pg >= 8:
                e.wait_ge(p.hs["t"], t30 + pg)
            for ii in range(n8):
                last = e.tensor_scalar(PTF8[:, ii * 128:(ii + 1) * 128], IOTAR[:],
                                       EXCM_CM[:, pg + ii:pg + ii + 1], None, ALU.is_equal)
            p.inc(last, "v", 1)

        @block.tensor
        def _(e, pg=pg, n8=n8):
            p.wait(e, "v", "q")
            for ii in range(n8):
                p.inc(e.matmul(PS[2][:, pg + ii:pg + ii + 1], PTF8[:, ii * 128:(ii + 1) * 128],
                               TIDM[:, pg + ii:pg + ii + 1], start=True, stop=True), "t", 1)

    @block.vector
    def _(e):
        p.wait(e, "t")
        p.chain(e, e.tensor_copy(TOSLM[:], PS[2][:, 0:NP2]))

    mark("p4_tos")
    for S in range(KT):
        t40 = p.cnt["t"]

        @block.vector
        def _(e, S=S):
            p.wait(e, "t")
            p.chain(e, e.tensor_scalar_add(SCOL[:], IOTAC[:], float(128 * S)))
            p.chain(e, e.tensor_scalar(INDS[:], OREP[:], SCOL[:, 0:1], None, ALU.is_le))
            p.chain(e, e.reduce_sum(PCOL[:], INDS[:], axis=AX.X))
            p.chain(e, e.tensor_scalar_add(PCOL[:], PCOL[:], -1.0))
            p.chain(e, e.tensor_tensor(INDS[:], INDS[:], OREP[:], ALU.mult))
            p.chain(e, e.reduce_max(OOFP[:], INDS[:], axis=AX.X))
            p.chain(e, e.tensor_tensor(KCOL[:], SCOL[:], OOFP[:], ALU.subtract))
            p.chain(e, e.tensor_scalar(AS_[:, 0:NP2], IOTAR[:, 0:NP2], PCOL[:, 0:1], None, ALU.is_equal))

        @block.tensor
        def _(e, S=S, t40=t40):
            p.wait(e, "v", "q")
            p.inc(e.transpose(PS[0][0:1, 0:128], KCOL[:], IDENTF[:]), "t", 1)

        @block.scalar
        def _(e, S=S, t40=t40):
            e.wait_ge(p.hs["t"], t40 + 1)
            p.inc(e.activation(KROWS[0:1, 0:128], PS[0][0:1, 0:128], AF.Copy), "a", 1)

        @block.tensor
        def _(e, S=S, t40=t40):
            p.wait(e, "a")
            p.inc(e.matmul(PS[1][:, 0:128], ONECOLF[0:1, 0:128], KROWS[0:1, 0:128],
                           start=True, stop=True), "t", 1)

        @block.vector
        def _(e, S=S, t40=t40):
            e.wait_ge(p.hs["t"], t40 + 2)
            p.chain(e, e.tensor_scalar(BST[:], PS[1][:, 0:128], IOTAC[:, 0:1], None, ALU.is_equal))

        @block.tensor
        def _(e, S=S, t40=t40):
            p.wait(e, "v", "q")
            p.inc(e.matmul(PS[3][:, 0:NP2], BST[:], TOSLM[:], start=True, stop=True), "t", 1)

        @block.vector
        def _(e, S=S, t40=t40):
            e.wait_ge(p.hs["t"], t40 + 3)
            p.chain(e, e.tensor_tensor(CSC[:], AS_[:, 0:NP2], PS[3][:, 0:NP2], ALU.mult))
            p.chain(e, e.reduce_sum(TOSF[:, S:S + 1], CSC[:], axis=AX.X))
            with nc.allow_low_precision(reason="token ids are exact ints < 16384"):
                p.chain(e, e.tensor_copy(TOSI[:, S:S + 1], TOSF[:, S:S + 1]))

        @block.gpsimd
        def _(e, S=S):
            p.wait(e, "v", "q")
            p.inc(e.indirect_dma_start(
                out=CTM[:, S * D:(S + 1) * D], out_offset=None,
                in_=yln.ap(), in_offset=IndirectOffsetOnAxis(TOSI[:, S:S + 1], 0),
                bounds_check=NTOK - 1, oob_is_err=False), "dA", 16)

    mark("p6_ln")
    dag0 = p.cnt["dA"] - 16 * KT
    for ci in range(KT):
        @block.vector
        def _(e, ci=ci, dag0=dag0):
            e.wait_ge(p.hs["dA"], dag0 + 16 * (ci + 1))
            if ci == 0:
                p.wait(e, "q")
            tile = CTM[:, ci * D:(ci + 1) * D]
            p.chain(e, e.reduce_sum(MEANS[:, ci:ci + 1], tile, axis=AX.X))
            p.chain(e, e.tensor_tensor(SQF[:], tile, tile, ALU.mult))
            p.chain(e, e.reduce_sum(RSTDS[:, ci:ci + 1], SQF[:], axis=AX.X))

    @block.vector
    def _(e):
        p.chain(e, e.tensor_scalar_mul(MEANS[:], MEANS[:], 1.0 / D))
        p.chain(e, e.tensor_scalar_mul(RSTDS[:], RSTDS[:], 1.0 / D))
        p.chain(e, e.tensor_tensor(SQF[:, 0:KT], MEANS[:], MEANS[:], ALU.mult))
        p.chain(e, e.tensor_tensor(RSTDS[:], RSTDS[:], SQF[:, 0:KT], ALU.subtract))
        p.inc(e.tensor_scalar_add(RSTDS[:], RSTDS[:], LN_EPS), "v", 1)

    @block.scalar
    def _(e):
        p.wait(e, "v", "q")
        e.activation(RSTDS[:], RSTDS[:], AF.Ln)
        p.inc(e.activation(RSTDS[:], RSTDS[:], AF.Exp, scale=-0.5), "a", 1)

    for ci in range(KT):
        @block.vector
        def _(e, ci=ci):
            if ci == 0:
                p.wait(e, "a", "q")
            else:
                e.wait_ge(p.hs["a"], p.cnt["a"])
            tile = CTM[:, ci * D:(ci + 1) * D]
            e.tensor_scalar(SQF[:], tile, MEANS[:, ci:ci + 1], None, ALU.subtract)
            e.tensor_scalar(SQF[:], SQF[:], RSTDS[:, ci:ci + 1], None, ALU.mult)
            e.tensor_tensor(SQF[:], SQF[:], GAMB[:], ALU.mult)
            p.inc(e.tensor_tensor(tile, SQF[:], BETB[:], ALU.add), "v", 1)

        tc0 = p.cnt["t"]; ac0 = p.cnt["a"]

        @block.tensor
        def _(e, ci=ci, tc0=tc0, ac0=ac0):
            p.wait(e, "v")
            for dc in range(NCH):
                if dc >= 2:
                    e.wait_ge(p.hs["a"], ac0 + dc - 1)
                p.inc(e.transpose(PSB2[dc % 2][:, 0:128],
                                  CTM[:, ci * D + dc * 128: ci * D + dc * 128 + 128], IDENT[:]), "t", 1)

        @block.scalar
        def _(e, ci=ci, tc0=tc0):
            for dc in range(NCH):
                e.wait_ge(p.hs["t"], tc0 + dc + 1)
                p.inc(e.activation(YCT[:, dc * K + ci * 128: dc * K + ci * 128 + 128],
                                   PSB2[dc % 2][:, 0:128], AF.Copy), "a", 1)

    mark("p7_ffn")
    for tk in range(NTK):
        a50 = p.cnt["a"]; v50 = p.cnt["v"]
        for fc in range(NFC):
            if fc % 2 == 0:
                @block.gpsimd
                def _(e, tk=tk, fc=fc):
                    if fc >= 2 or tk > 0:
                        e.wait_ge(p.hs["t"], p.cnt["t"] - 1)
                    else:
                        p.wait(e, "v", "a", "q")
                    p.inc(e.dma_start(out=WF1C[(fc // 2) % 2][:, 0:2 * D].rearrange("p (c f) -> p c f", f=256),
                                      in_=fc1T.ap()[0:D, fc * 128:(fc + 2) * 128].rearrange("(c p) f -> p c f", p=128)), "dG", 16)

            @block.tensor
            def _(e, tk=tk, fc=fc, a50=a50):
                p.wait(e, "dG")
                if fc >= 2:
                    e.wait_ge(p.hs["a"], a50 + 2 * (fc - 2) + 1)
                elif tk > 0:
                    p.wait(e, "a")
                for dc in range(NCH):
                    last = e.matmul(PS[fc % 2][:, 0:TWK],
                                    WF1C[(fc // 2) % 2][:, dc * 256 + (fc % 2) * 128: dc * 256 + (fc % 2) * 128 + 128],
                                    YCT[:, dc * K + tk * TWK: dc * K + tk * TWK + TWK],
                                    start=(dc == 0), stop=(dc == NCH - 1))
                p.inc(last, "t", 1)

            @block.scalar
            def _(e, tk=tk, fc=fc, a50=a50, v50=v50):
                p.wait(e, "t")
                if fc >= 2:
                    e.wait_ge(p.hs["v"], v50 + 2 * (fc - 2) + 2)
                p.inc(e.activation(FFT[fc % 2][:], PS[fc % 2][:, 0:TWK], AF.Identity,
                                   bias=B1S[:, fc:fc + 1]), "a", 1)

            @block.vector
            def _(e, fc=fc, a50=a50):
                e.wait_ge(p.hs["a"], a50 + 2 * fc + 1)
                e.tensor_tensor(UU[fc % 2][:], FFT[fc % 2][:], FFT[fc % 2][:], ALU.mult)
                e.tensor_scalar(UU[fc % 2][:], UU[fc % 2][:], G1, 1.0, ALU.mult, ALU.add)
                p.inc(e.tensor_tensor(UU[fc % 2][:], UU[fc % 2][:], FFT[fc % 2][:], ALU.mult), "v", 1)

            @block.scalar
            def _(e, fc=fc, v50=v50):
                e.wait_ge(p.hs["v"], v50 + 2 * fc + 1)
                p.inc(e.activation(UU[fc % 2][:], UU[fc % 2][:], AF.Tanh, scale=G0), "a", 1)

            @block.vector
            def _(e, tk=tk, fc=fc, a50=a50):
                e.wait_ge(p.hs["a"], a50 + 2 * fc + 2)
                e.tensor_scalar(UU[fc % 2][:], UU[fc % 2][:], 0.5, 0.5, ALU.mult, ALU.add)
                with nc.allow_low_precision(reason="bf16 matmul operand"):
                    p.inc(e.tensor_tensor(GB2[:, fc * TWK:(fc + 1) * TWK], UU[fc % 2][:],
                                          FFT[fc % 2][:], ALU.mult), "v", 1)

        for half in range(2):
            for fc in range(NFC):
                if fc % 4 == 0:
                    @block.gpsimd
                    def _(e, fc=fc, half=half):
                        e.wait_ge(p.hs["t"], p.cnt["t"] - 1)
                        p.inc(e.dma_start(out=WF2C[(fc // 4) % 2][:, 0:2048].rearrange("p (c f) -> p c f", f=512),
                                          in_=fc2T.ap()[fc * 128:(fc + 4) * 128, half * 512:(half + 1) * 512].rearrange("(c p) f -> p c f", p=128)), "dG", 16)

                @block.tensor
                def _(e, fc=fc, half=half, tk=tk):
                    p.wait(e, "dG")
                    if fc == 0:
                        p.wait(e, "v", "a")
                    for oc4 in range(4):
                        last = e.matmul(PS[2 + oc4][:, 0:TWK],
                                        WF2C[(fc // 4) % 2][:, (fc % 4) * 512 + oc4 * 128: (fc % 4) * 512 + (oc4 + 1) * 128],
                                        GB2[:, fc * TWK:(fc + 1) * TWK],
                                        start=(fc == 0), stop=(fc == NFC - 1))
                    p.inc(last, "t", 1)

            for oc4 in range(4):
                oc = half * 4 + oc4

                @block.scalar
                def _(e, oc=oc, oc4=oc4, tk=tk):
                    p.wait(e, "t")
                    if oc4 == 0 and tk > 0:
                        p.wait(e, "dA")
                    if oc4 >= 2:
                        e.wait_ge(p.hs["t"], p.cnt["t"])
                    p.inc(e.activation(FB[oc4 % 2][:, 0:TWK], PS[2 + oc4][:, 0:TWK], AF.Identity,
                                       bias=B2S[:, oc:oc + 1]), "a", 1)

                tq0 = p.cnt["t"]; aq0 = p.cnt["a"]

                @block.tensor
                def _(e, oc=oc, oc4=oc4, tq0=tq0, aq0=aq0):
                    p.wait(e, "a")
                    for q in range(QK):
                        if q >= 2:
                            e.wait_ge(p.hs["a"], aq0 + q - 1)
                        p.inc(e.transpose(PSB2[q % 2][:, 0:128],
                                          FB[oc4 % 2][:, q * 128: q * 128 + 128], IDENT[:]), "t", 1)

                @block.scalar
                def _(e, oc=oc, tq0=tq0):
                    for q in range(QK):
                        e.wait_ge(p.hs["t"], tq0 + q + 1)
                        p.inc(e.activation(OTM[:, q * D + oc * 128: q * D + oc * 128 + 128],
                                           PSB2[q % 2][:, 0:128], AF.Copy), "a", 1)

        @block.gpsimd
        def _(e, tk=tk):
            p.wait(e, "a")
            if tk == 0:
                p.wait(e, "dS")
            for q in range(QK):
                p.inc(e.indirect_dma_start(
                    out=plane.ap(), out_offset=IndirectOffsetOnAxis(TOSI[:, tk * QK + q: tk * QK + q + 1], 0),
                    in_=OTM[:, q * D:(q + 1) * D], in_offset=None,
                    bounds_check=NTOK - 1, oob_is_err=False), "dA", 16)

    mark(None)
    @block.sync
    def _(e):
        p.wait(e, "dS", "dG", "dA")

    @block.vector
    def _(e):
        p.wait(e, "t", "a", "g", "q")

    return nc


def build_launch3(T=2048):
    nc = bass.Bass("TRN2", target_bir_lowering=False, detect_race_conditions=False)
    NT = T // 128

    def ein(name, shape, dt=F32):
        return nc.dram_tensor(name, list(shape), dt, kind="ExternalInput")

    x3T = ein("x3T", (D, T))
    ylnT = ein("ylnT", (D, T), BF16)
    pls = [ein(f"pl{e}", (T, D), BF16) for e in range(NE)]
    kwc = ein("kwc", (128, NE * NT))
    out_tm = nc.dram_tensor("out_tm", [T, D], F32, kind="ExternalOutput")

    identf_c = nc.inline_tensor(np.eye(128).astype(np.float32), name="id_f32")

    ctx = ExitStack()
    block = ctx.enter_context(nc.Block())
    p = Prog(nc, block, ctx)

    IDENTF = ctx.enter_context(nc.sbuf_tensor("IDENTff", [128, 128], F32))
    X3F = [ctx.enter_context(nc.sbuf_tensor(f"X3F{i}", [128, D], F32)) for i in range(2)]
    YLT = [ctx.enter_context(nc.sbuf_tensor(f"YLT{i}", [128, D], BF16)) for i in range(2)]
    PLT = [ctx.enter_context(nc.sbuf_tensor(f"PLT{i}", [128, D], BF16)) for i in range(2)]
    ACC = [ctx.enter_context(nc.sbuf_tensor(f"ACC{i}", [128, D], F32)) for i in range(2)]
    KWS = ctx.enter_context(nc.sbuf_tensor("KWSf", [128, NE * NT], F32))
    PS = [ctx.enter_context(nc.psum_tensor(f"PS{i}", [128, 512], F32)) for i in range(2)]

    @block.sync
    def _(e):
        p.inc(e.dma_start(out=IDENTF[:], in_=identf_c.ap()), "dS", 16)
        p.inc(e.dma_start(out=KWS[:], in_=kwc.ap()), "dS", 16)

    vend = {}
    aend = {}
    dend = {}
    for j in range(NT):
        tj0 = p.cnt["t"]; vj0 = p.cnt["v"]; aj0 = p.cnt["a"]
        gj0 = p.cnt["dG"]; dj0 = p.cnt["dS"]

        @block.sync
        def _(e, j=j, dj0=dj0):
            if j >= 2:
                e.wait_ge(p.hs["v"], vend[j - 2])
                e.wait_ge(p.hs["dS"], dend[j - 2])
            p.inc(e.dma_start(out=X3F[j % 2][:].rearrange("p (c t) -> p c t", t=128),
                              in_=x3T.ap()[:, j * 128:(j + 1) * 128].rearrange("(c p) t -> p c t", p=128)), "dS", 16)
            p.inc(e.dma_start(out=YLT[j % 2][:], in_=ylnT.ap()[:, j * 128:(j + 1) * 128],
                              transpose=True), "dS", 16)

        @block.gpsimd
        def _(e, j=j, vj0=vj0):
            for ee in range(NE):
                if ee >= 2:
                    e.wait_ge(p.hs["v"], vj0 + ee)
                elif j > 0:
                    e.wait_ge(p.hs["v"], vend[j - 1])
                p.inc(e.dma_start(out=PLT[ee % 2][:], in_=pls[ee].ap()[j * 128:(j + 1) * 128, :]), "dG", 16)

        @block.tensor
        def _(e, j=j, dj0=dj0):
            e.wait_ge(p.hs["dS"], dj0 + 16)
            if j >= 1:
                e.wait_ge(p.hs["a"], aend[j - 1])
            for dc in range(8):
                p.inc(e.transpose(PS[dc // 4][:, (dc % 4) * 128:(dc % 4) * 128 + 128],
                                  X3F[j % 2][:, dc * 128:(dc + 1) * 128], IDENTF[:]), "t", 1)

        @block.scalar
        def _(e, j=j, tj0=tj0):
            e.wait_ge(p.hs["t"], tj0 + 4)
            p.inc(e.activation(ACC[j % 2][:, 0:512], PS[0][:, 0:512], AF.Copy), "a", 1)
            e.wait_ge(p.hs["t"], tj0 + 8)
            p.inc(e.activation(ACC[j % 2][:, 512:1024], PS[1][:, 0:512], AF.Copy), "a", 1)

        @block.vector
        def _(e, j=j, aj0=aj0, gj0=gj0, dj0=dj0):
            e.wait_ge(p.hs["a"], aj0 + 2)
            e.wait_ge(p.hs["dS"], dj0 + 32)
            p.inc(e.tensor_tensor(ACC[j % 2][:], ACC[j % 2][:], YLT[j % 2][:], ALU.add), "v", 1)
            for ee in range(NE):
                e.wait_ge(p.hs["dG"], gj0 + 16 * (ee + 1))
                p.inc(e.scalar_tensor_tensor(ACC[j % 2][:], PLT[ee % 2][:],
                                             KWS[:, ee * NT + j: ee * NT + j + 1],
                                             ACC[j % 2][:], ALU.mult, ALU.add), "v", 1)

        @block.sync
        def _(e, j=j, vj0=vj0):
            e.wait_ge(p.hs["v"], vj0 + 9)
            p.inc(e.dma_start(out=out_tm.ap()[j * 128:(j + 1) * 128, :], in_=ACC[j % 2][:]), "dS", 16)

        vend[j] = p.cnt["v"]
        aend[j] = p.cnt["a"]
        dend[j] = p.cnt["dS"]

    @block.sync
    def _(e):
        p.wait(e, "dS", "dG", "dA")

    @block.vector
    def _(e):
        p.wait(e, "t", "a", "g", "q")

    return nc


from concourse.bass_utils import run_bass_kernel_spmd

_T = 2048
_NC = 8


def _v128(v):
    return np.ascontiguousarray(np.asarray(v, dtype=np.float32).reshape(-1, 128).T)


def kernel(**inputs):
    g = {k: np.asarray(v) for k, v in inputs.items()}
    x = g["L0_attn_x"].astype(np.float32)

    nc1 = build_launch1(T=_T)
    common = {
        "Wq0": g["L0_attn_Wq"], "Wk0": g["L0_attn_Wk"], "Wv0": g["L0_attn_Wv"],
        "Wg0": g["L0_attn_Wg"], "Wo0": g["L0_attn_Wo"],
        "W1": g["L0_attn_W1"], "W2": g["L0_attn_W2"],
        "ln1w": _v128(g["L0_attn_ln1_w"]), "ln1b": _v128(g["L0_attn_ln1_b"]),
        "ln2w": _v128(g["L0_attn_ln2_w"]), "ln2b": _v128(g["L0_attn_ln2_b"]),
        "bgv": _v128(g["L0_attn_bg"]), "b1v": _v128(g["L0_attn_b1"]), "b2v": _v128(g["L0_attn_b2"]),
        "WrT": np.ascontiguousarray(g["L0_ffn_W_r"].T),
        "WkTr": np.ascontiguousarray(g["L0_ffn_W_k"].T),
        "WvTr": np.ascontiguousarray(g["L0_ffn_W_v"].T),
        "murv": _v128(g["L0_ffn_mu_r"]), "mukv": _v128(g["L0_ffn_mu_k"]),
        "Wq1": g["L1_attn_Wq"], "Wkv1": g["L1_attn_Wkv"], "Wo1": g["L1_attn_Wo"],
        "bo1v": _v128(g["L1_attn_bo"]),
        "gate_wT": np.ascontiguousarray(g["L1_ffn_gate_w"].T),
    }
    in1 = [dict(common, xT=np.ascontiguousarray(x[c].T)) for c in range(_NC)]
    res1 = run_bass_kernel_spmd(nc1, in1, core_ids=list(range(_NC)))
    x3Ts = [np.asarray(res1.results[c]["x3T"]) for c in range(_NC)]
    ylnTs = [np.asarray(res1.results[c]["ylnT"]) for c in range(_NC)]
    scs = [np.asarray(res1.results[c]["scoresT"]) for c in range(_NC)]

    yln_full = np.ascontiguousarray(np.concatenate([ylnTs[c].T for c in range(_NC)], axis=0))
    scores_full = np.concatenate(scs, axis=1)

    nc2 = build_launch2(NTOK=_NC * _T, K=_NC * _T // 8)
    gamma_r = np.ascontiguousarray(g["L1_ffn_ln_gamma"].astype(np.float32).reshape(1, D))
    beta_r = np.ascontiguousarray(g["L1_ffn_ln_beta"].astype(np.float32).reshape(1, D))
    in2 = []
    for e in range(NE):
        in2.append({
            "yln": yln_full,
            "sc_pm": np.ascontiguousarray(scores_full[e].reshape(128, 128)),
            "fc1T": np.ascontiguousarray(g["L1_ffn_fc1s"][e].T),
            "fc2T": np.ascontiguousarray(g["L1_ffn_fc2s"][e].T),
            "b1c": np.ascontiguousarray(g["L1_ffn_b1s"][e].reshape(FFN // 128, 128).T),
            "b2c": np.ascontiguousarray(g["L1_ffn_b2s"][e].reshape(8, 128).T),
            "gamma_r": gamma_r, "beta_r": beta_r,
        })
    res2 = run_bass_kernel_spmd(nc2, in2, core_ids=list(range(_NC)))
    planes = [np.asarray(res2.results[e]["plane"]) for e in range(NE)]
    kwcms = [np.asarray(res2.results[e]["kw_cm"]) for e in range(NE)]

    nc3 = build_launch3(T=_T)
    NT3 = _T // 128
    in3 = []
    for c in range(_NC):
        kwc = np.ascontiguousarray(
            np.concatenate([kwcms[e][:, c * NT3:(c + 1) * NT3] for e in range(NE)], axis=1))
        m = {"x3T": x3Ts[c], "ylnT": ylnTs[c], "kwc": kwc}
        for e in range(NE):
            m[f"pl{e}"] = np.ascontiguousarray(planes[e][c * _T:(c + 1) * _T, :])
        in3.append(m)
    res3 = run_bass_kernel_spmd(nc3, in3, core_ids=list(range(_NC)))

    out = np.stack([np.asarray(res3.results[c]["out_tm"]) for c in range(_NC)], axis=0)
    return out.astype(np.float32)
